# version 02 (final)
# baseline (speedup 1.0000x reference)
import numpy as np
import concourse.bass as bass
import concourse.mybir as mybir
from concourse.bass_utils import run_bass_kernel_spmd
from concourse.bass import IndirectOffsetOnAxis

F32 = mybir.dt.float32
BF16 = mybir.dt.bfloat16
U8 = mybir.dt.uint8
U16 = mybir.dt.uint16
U32 = mybir.dt.uint32
I16 = mybir.dt.int16
I32 = mybir.dt.int32
AF = mybir.ActivationFunctionType
ALU = mybir.AluOpType
AX = mybir.AxisListType

B, T, D = 4, 4096, 1024
NCORES = 8
TPC = T // 2
NTT = TPC // 128
NCH = D // 128


LANES = {
    "pe": ("tensor", 1),
    "dve": ("vector", 1),
    "act": ("scalar", 1),
    "pool": ("gpsimd", 1),
    "sp": ("sync", 16),
    "actd": ("scalar", 16),
    "poold": ("gpsimd", 16),
}


class _Op:
    __slots__ = ("lane", "fn", "reads", "writes", "li", "deps", "need_inc",
                 "inc_idx", "tag", "dep_target")

    def __init__(self, lane, fn, reads, writes, tag):
        self.lane = lane
        self.fn = fn
        self.reads = reads
        self.writes = writes
        self.tag = tag
        self.li = -1
        self.deps = []
        self.need_inc = False
        self.dep_target = False
        self.inc_idx = -1


def _ap(x):
    if x is None or isinstance(x, (int, float)):
        return x
    if hasattr(x, "tensor"):
        return x
    try:
        return x[:]
    except Exception:
        return x


def _keys(x):
    if x is None:
        return []
    if isinstance(x, (list, tuple)):
        out = []
        for e in x:
            out.extend(_keys(e))
        return out
    if hasattr(x, "tensor"):
        return [x.tensor.name]
    if hasattr(x, "name"):
        return [x.name]
    return [x]


class Prog:
    def __init__(self, nc, name="prog"):
        self.nc = nc
        self.name = name
        self.ops = []
        self._stack = []
        self.tensors = {}
        self._const_cnt = 0
        self.out_dma_lanes = set()

    def sb(self, name, shape, dtype):
        if name in self.tensors:
            return self.tensors[name]
        g = self.nc.sbuf_tensor(name, shape, dtype)
        h = g.__enter__()
        self._stack.append(g)
        self.tensors[name] = h
        return h

    def ps(self, name, shape, dtype=F32):
        if name in self.tensors:
            return self.tensors[name]
        g = self.nc.psum_tensor(name, shape, dtype)
        h = g.__enter__()
        self._stack.append(g)
        self.tensors[name] = h
        return h

    def dram(self, name, shape, dtype, kind="Internal"):
        return self.nc.dram_tensor(name, shape, dtype, kind=kind)

    def const_dram(self, arr):
        self._const_cnt += 1
        return self.nc.inline_tensor(np.ascontiguousarray(arr),
                                     name=f"c{self._const_cnt}_{self.name}")

    def const_sb(self, name, arr, dtype):
        if dtype == BF16:
            import ml_dtypes
            arr = np.asarray(arr).astype(ml_dtypes.bfloat16)
        elif dtype == F32:
            arr = np.asarray(arr).astype(np.float32)
        elif dtype in (I16,):
            arr = np.asarray(arr).astype(np.int16)
        elif dtype in (U16,):
            arr = np.asarray(arr).astype(np.uint16)
        elif dtype in (I32,):
            arr = np.asarray(arr).astype(np.int32)
        d = self.const_dram(arr)
        t = self.sb(name, list(arr.shape), dtype)
        self.dma(t[:], d.ap(), lane="sp")
        return t

    def add(self, lane, fn, reads=(), writes=(), tag=""):
        op = _Op(lane, fn, _keys(reads), _keys(writes), tag)
        self.ops.append(op)
        return op

    def dma(self, dst, src, lane="sp", reads=(), writes=(), **kw):
        dst, src = _ap(dst), _ap(src)
        eng = getattr(self.nc, LANES[lane][0])
        self.add(lane, lambda: eng.dma_start(out=dst, in_=src, **kw),
                 reads=[src] + _keys(reads), writes=[dst] + _keys(writes),
                 tag="dma")

    def dma_out(self, dst, src, lane="sp", **kw):
        dst, src = _ap(dst), _ap(src)
        eng = getattr(self.nc, LANES[lane][0])
        op = self.add(lane, lambda: eng.dma_start(out=dst, in_=src, **kw),
                      reads=[src], writes=[dst], tag="dma_out")
        op.need_inc = True
        self.out_dma_lanes.add(lane)

    def mm(self, out, lhsT, rhs, start=True, stop=True):
        out, lhsT, rhs = _ap(out), _ap(lhsT), _ap(rhs)
        self.add("pe", lambda: self.nc.tensor.matmul(out, lhsT, rhs,
                                                     start=start, stop=stop),
                 reads=[lhsT, rhs], writes=[out], tag="mm")

    def transp(self, out_ps, in_sb, ident):
        out_ps, in_sb, ident = _ap(out_ps), _ap(in_sb), _ap(ident)
        self.add("pe", lambda: self.nc.tensor.transpose(out_ps, in_sb, ident),
                 reads=[in_sb, ident], writes=[out_ps], tag="tr")

    def float_col(self, v):
        key = ("fcol", float(v))
        if key in self.tensors:
            return self.tensors[key][:]
        t = self.sb(f"fcol_{len(self.tensors)}", [128, 1], F32)
        self.memset(t[:], float(v))
        self.tensors[key] = t
        return t[:]

    def act(self, out, in_, func, bias=0.0, scale=1.0, accum_out=None,
            extra_reads=()):
        out, in_ = _ap(out), _ap(in_)
        if isinstance(bias, float) and bias != 0.0:
            bias = self.float_col(bias)
        if accum_out is not None:
            accum_out = _ap(accum_out)
        w = [out] if accum_out is None else [out, accum_out]
        r = [in_] + _keys(extra_reads)
        if hasattr(bias, "tensor"):
            r.append(bias)
        if hasattr(scale, "tensor"):
            r.append(scale)
        self.add("act", lambda: self.nc.scalar.activation(
            out=out, in_=in_, func=func, bias=bias, scale=scale,
            accum_out=accum_out), reads=r, writes=w, tag="act")

    def copy(self, out, in_, lane="dve"):
        out, in_ = _ap(out), _ap(in_)
        eng = getattr(self.nc, LANES[lane][0])
        if lane == "act":
            self.add(lane, lambda: self.nc.scalar.copy(out=out, in_=in_),
                     reads=[in_], writes=[out], tag="copy")
        else:
            self.add(lane, lambda: eng.tensor_copy(out=out, in_=in_),
                     reads=[in_], writes=[out], tag="copy")

    def memset(self, ap, v, lane="dve"):
        ap = _ap(ap)
        eng = getattr(self.nc, LANES[lane][0])
        self.add(lane, lambda: eng.memset(ap, v), writes=[ap], tag="memset")

    def tt(self, out, a, b, op, lane="dve"):
        out, a, b = _ap(out), _ap(a), _ap(b)
        eng = getattr(self.nc, LANES[lane][0])
        self.add(lane, lambda: eng.tensor_tensor(out=out, in0=a, in1=b, op=op),
                 reads=[a, b], writes=[out], tag="tt")

    def ts(self, out, in0, s1, s2=None, op0=ALU.mult, op1=ALU.bypass,
           accum_out=None, lane="dve"):
        out, in0 = _ap(out), _ap(in0)
        s1, s2 = _ap(s1) if s1 is not None else s1, _ap(s2) if s2 is not None else s2
        if accum_out is not None:
            accum_out = _ap(accum_out)
        eng = getattr(self.nc, LANES[lane][0])
        r = [in0]
        if hasattr(s1, "tensor"):
            r.append(s1)
        if hasattr(s2, "tensor"):
            r.append(s2)
        w = [out] if accum_out is None else [out, accum_out]
        self.add(lane, lambda: eng.tensor_scalar(
            out=out, in0=in0, scalar1=s1, scalar2=s2, op0=op0, op1=op1,
            accum_out=accum_out), reads=r, writes=w, tag="ts")

    def stt(self, out, in0, scalar, in1, op0, op1, accum_out=None, lane="dve"):
        out, in0, in1, scalar = _ap(out), _ap(in0), _ap(in1), _ap(scalar)
        if accum_out is not None:
            accum_out = _ap(accum_out)
        eng = getattr(self.nc, LANES[lane][0])
        r = [in0, in1]
        if hasattr(scalar, "tensor"):
            r.append(scalar)
        w = [out] if accum_out is None else [out, accum_out]
        self.add(lane, lambda: eng.scalar_tensor_tensor(
            out=out, in0=in0, scalar=scalar, in1=in1, op0=op0, op1=op1,
            accum_out=accum_out), reads=r, writes=w, tag="stt")

    def red(self, out, in_, op, negate=None):
        out, in_ = _ap(out), _ap(in_)
        self.add("dve", lambda: self.nc.vector.tensor_reduce(
            out=out, in_=in_, axis=AX.X, op=op, negate=negate),
            reads=[in_], writes=[out], tag="red")

    def recip(self, out, in_):
        out, in_ = _ap(out), _ap(in_)
        self.add("dve", lambda: self.nc.vector.reciprocal(out=out, in_=in_),
                 reads=[in_], writes=[out], tag="recip")

    def vmax8(self, out, in_):
        out, in_ = _ap(out), _ap(in_)
        self.add("dve", lambda: self.nc.vector.max(out=out, in_=in_),
                 reads=[in_], writes=[out], tag="max8")

    def vmaxidx(self, out_idx, in_max, in_values):
        out_idx, in_max, in_values = _ap(out_idx), _ap(in_max), _ap(in_values)
        self.add("dve", lambda: self.nc.vector.max_index(
            out=out_idx, in_max=in_max, in_values=in_values),
            reads=[in_max, in_values], writes=[out_idx], tag="maxidx")

    def matchrep(self, out, in_to_replace, in_values, imm):
        out = _ap(out)
        in_to_replace, in_values = _ap(in_to_replace), _ap(in_values)
        self.add("dve", lambda: self.nc.vector.match_replace(
            out=out, in_to_replace=in_to_replace, in_values=in_values,
            imm_value=imm), reads=[in_to_replace, in_values], writes=[out],
            tag="matchrep")

    def select(self, out, mask, on_true, on_false):
        out, mask = _ap(out), _ap(mask)
        on_true, on_false = _ap(on_true), _ap(on_false)
        self.add("dve", lambda: self.nc.vector.select(
            out=out, mask=mask, on_true=on_true, on_false=on_false),
            reads=[mask, on_true, on_false], writes=[out], tag="select")

    def iota(self, out, pattern, base=0, mult=0):
        out = _ap(out)
        self.add("pool", lambda: self.nc.gpsimd.iota(
            out, pattern, base=base, channel_multiplier=mult,
            allow_small_or_imprecise_dtypes=True),
            writes=[out], tag="iota")

    def affsel(self, out, in_, pattern, cmp, fill, base=0, mult=0):
        out, in_ = _ap(out), _ap(in_)
        self.add("pool", lambda: self.nc.gpsimd.affine_select(
            out, in_, pattern, cmp, fill, base=base, channel_multiplier=mult),
            reads=[in_], writes=[out], tag="affsel")

    def scan_sum(self, out, data0, data1, initial=0.0):
        out, data0, data1 = _ap(out), _ap(data0), _ap(data1)
        """out[i] = cumsum along free of data0 (data1 ignored via bypass)."""
        self.add("dve", lambda: self.nc.vector.tensor_tensor_scan(
            out=out, data0=data0, data1=data1, initial=initial,
            op0=ALU.add, op1=ALU.bypass),
            reads=[data0, data1] + ([initial] if hasattr(initial, "tensor") else []),
            writes=[out], tag="scan")

    def indirect_gather(self, out, table, idx_ap):
        out, table, idx_ap = _ap(out), _ap(table), _ap(idx_ap)
        self.add("poold", lambda: self.nc.gpsimd.indirect_dma_start(
            out=out, out_offset=None, in_=table,
            in_offset=IndirectOffsetOnAxis(ap=idx_ap, axis=0)),
            reads=[table, idx_ap], writes=[out], tag="igather")

    SELF_SYNC = ("dve", "act", "pool")

    def build(self, sync_same_engine=True):
        nc = self.nc
        lane_ops = {ln: [] for ln in LANES}
        for op in self.ops:
            op.li = len(lane_ops[op.lane])
            lane_ops[op.lane].append(op)

        last_writer = {}
        last_access = {}
        nlanes = list(LANES)
        lid = {ln: i for i, ln in enumerate(nlanes)}
        NL = len(nlanes)
        clocks = {}

        def clock_of(op):
            return clocks[op]

        lane_prev = {ln: None for ln in LANES}
        for op in self.ops:
            deps = []
            for k in op.reads:
                w = last_writer.get(k)
                if w is not None:
                    deps.append(w)
            for k in op.writes:
                w = last_writer.get(k)
                if w is not None:
                    deps.append(w)
                for a in last_access.get(k, ()):
                    deps.append(a)
            prev = lane_prev[op.lane]
            cl = list(clocks[prev]) if prev is not None else [0] * NL
            if prev is not None and op.lane not in self.SELF_SYNC:
                cl[lid[prev.lane]] = max(cl[lid[prev.lane]], prev.li + 1)
            kept = []
            bymax = {}
            for d in deps:
                if d.lane == op.lane and not (
                        sync_same_engine and op.lane in self.SELF_SYNC):
                    continue
                if d is op:
                    continue
                cur = bymax.get(d.lane)
                if cur is None or d.li > cur.li:
                    bymax[d.lane] = d
            for d in bymax.values():
                if cl[lid[d.lane]] >= d.li + 1:
                    continue
                kept.append(d)
                d.need_inc = True
                d.dep_target = True
                dcl = clocks[d]
                for i in range(NL):
                    if dcl[i] > cl[i]:
                        cl[i] = dcl[i]
                cl[lid[d.lane]] = max(cl[lid[d.lane]], d.li + 1)
            op.deps = kept
            clocks[op] = tuple(cl)
            lane_prev[op.lane] = op
            for k in op.writes:
                last_writer[k] = op
                last_access[k] = []
            for k in op.reads:
                last_access.setdefault(k, []).append(op)

        inc_count = {ln: 0 for ln in LANES}
        for op in self.ops:
            if LANES[op.lane][1] == 16:
                op.need_inc = True
            if op.need_inc:
                inc_count[op.lane] += 1
                op.inc_idx = inc_count[op.lane]

        sems = {}
        for ln in LANES:
            if inc_count[ln] > 0:
                g = nc.semaphore(f"s_{ln}")
                sems[ln] = g.__enter__()
                self._stack.append(g)

        blk_guard = nc.Block()
        blk = blk_guard.__enter__()
        self._stack.append(blk_guard)

        eng_lanes = {}
        for ln in LANES:
            eng_lanes.setdefault(LANES[ln][0], []).append(ln)

        for eng_name, lns in eng_lanes.items():
            ops = [op for op in self.ops if op.lane in lns]
            if not ops:
                continue
            use_lns = set(op.lane for op in ops)

            def body(eng, ops=ops, lns=use_lns):
                for op in ops:
                    for d in op.deps:
                        eng.wait_ge(sems[d.lane], d.inc_idx * LANES[d.lane][1])
                    inst = op.fn()
                    if op.need_inc:
                        inst.then_inc(sems[op.lane], LANES[op.lane][1])
                        if LANES[op.lane][1] == 16 and op.dep_target:
                            eng.wait_ge(sems[op.lane],
                                        op.inc_idx * LANES[op.lane][1])
                for ln in lns:
                    if ln in self.out_dma_lanes:
                        eng.wait_ge(sems[ln], inc_count[ln] * LANES[ln][1])
            getattr(blk, eng_name)(body)

        blk_guard.__exit__(None, None, None)
        self._stack.pop()
        return self

    def finish(self):
        while self._stack:
            self._stack.pop().__exit__(None, None, None)




def bcast_rows_f32(p, ones1, ps_region, row_f32, hi_sc, lo_sc, dst, w):
    p.copy(hi_sc[0:1, 0:w], row_f32)
    p.tt(lo_sc[0:1, 0:w], row_f32, hi_sc[0:1, 0:w], ALU.subtract)
    p.copy(lo_sc[0:1, w:2 * w], lo_sc[0:1, 0:w])
    p.mm(ps_region, ones1[:], hi_sc[0:1, 0:w], start=True, stop=False)
    p.mm(ps_region, ones1[:], lo_sc[0:1, w:2 * w], start=False, stop=True)
    p.copy(dst, ps_region)


def load_ident(p):
    ident = p.const_sb("ident_bf", np.eye(128, dtype=np.float32), BF16)
    identf = p.const_sb("ident_f32", np.eye(128, dtype=np.float32), F32)
    return ident, identf


def vec_to_pcol(p, name, hbm_vec, identf, n=D, dtype=F32):
    ncol = n // 128
    tmp = p.sb(name + "_tmp", [ncol, 128], F32)
    p.dma(tmp[0:ncol, :], hbm_vec.rearrange("(c q) -> c q", q=128))
    pst = p.ps("vpc_ps", [128, 16], F32)
    p.transp(pst[:, 0:ncol], tmp[0:ncol, :], identf[0:ncol, 0:ncol])
    out = p.sb(name, [128, ncol], dtype)
    p.copy(out[:], pst[:, 0:ncol])
    return out


def build_feat(p, x_hbm, ntok, ident, name="x", also_tok=None,
               want_err=False):
    nt = ntok // 128
    x_feat = p.sb(name + "_feat", [128, NCH * ntok], BF16)
    xt = p.sb(name + "_ttmp", [128, D], BF16) if also_tok is None else also_tok
    pst = p.ps("psTr_sh", [128, 1024], BF16)
    xe_feat = None
    if want_err:
        xe_feat = p.sb(name + "_efeat", [128, NCH * ntok], BF16)
        xf32 = p.sb(name + "_f32tmp", [128, D], F32)
        xet = p.sb(name + "_etmp", [128, D], BF16)
    for tt in range(nt):
        tok_ap = xt[:, tt * D:(tt + 1) * D] if also_tok is not None \
            else xt[:, 0:D]
        p.dma(tok_ap, x_hbm[tt * 128:(tt + 1) * 128, :], lane="poold")
        for c in range(NCH):
            p.transp(pst[:, c * 128:(c + 1) * 128],
                     tok_ap[:, c * 128:(c + 1) * 128], ident)
        for c in range(NCH):
            p.copy(x_feat[:, c * ntok + tt * 128: c * ntok + (tt + 1) * 128],
                   pst[:, c * 128:(c + 1) * 128])
        if want_err:
            p.dma(xf32[:], x_hbm[tt * 128:(tt + 1) * 128, :])
            p.tt(xet[:], xf32[:], tok_ap, ALU.subtract)
            for c in range(NCH):
                p.transp(pst[:, c * 128:(c + 1) * 128],
                         xet[:, c * 128:(c + 1) * 128], ident)
            for c in range(NCH):
                p.copy(xe_feat[:, c * ntok + tt * 128:
                               c * ntok + (tt + 1) * 128],
                       pst[:, c * 128:(c + 1) * 128])
    if want_err:
        return x_feat, xe_feat
    return x_feat


def layernorm_store(p, s_tiles, out_rows_fn, eps=1e-5):
    stat = p.sb("ln_stat", [128, 8], F32)
    sq = p.sb("ln_sq", [128, D], F32)
    for tt, s in s_tiles:
        mean = stat[:, 0:1]
        var = stat[:, 1:2]
        rstd = stat[:, 2:3]
        m2 = stat[:, 3:4]
        p.red(mean, s, ALU.add)
        p.ts(mean, mean, 1.0 / D, None, ALU.mult)
        p.tt(sq, s, s, ALU.mult)
        p.red(var, sq, ALU.add)
        p.tt(m2, mean, mean, ALU.mult)
        p.ts(var, var, 1.0 / D, m2, ALU.mult, ALU.subtract)
        p.act(rstd, var, AF.Sqrt, bias=float(eps), scale=1.0)
        p.recip(rstd, rstd)
        p.ts(sq, s, mean, rstd, ALU.subtract, ALU.mult)
        p.dma_out(out_rows_fn(tt), sq[:], lane="sp")



def build_chanmix_nc(exact=False):
    nc = new_nc()
    x = nc.dram_tensor("x_my", [TPC, D], F32, kind="ExternalInput").ap()
    halo_in = nc.dram_tensor("halo_in", [1, D], F32, kind="ExternalInput").ap()
    Wr = nc.dram_tensor("W_r", [D, D], F32, kind="ExternalInput").ap()
    Wk = nc.dram_tensor("W_k", [D, D], F32, kind="ExternalInput").ap()
    Wv = nc.dram_tensor("W_v", [D, D], F32, kind="ExternalInput").ap()
    mur_h = nc.dram_tensor("mu_r", [D], F32, kind="ExternalInput").ap()
    muk_h = nc.dram_tensor("mu_k", [D], F32, kind="ExternalInput").ap()
    xo = nc.dram_tensor("xo_my", [TPC, D], F32, kind="ExternalOutput").ap()

    p = Prog(nc, "chanmix")
    ident, identf = load_ident(p)
    HT = TPC // 2
    mur = vec_to_pcol(p, "mur", mur_h, identf)
    muk = vec_to_pcol(p, "muk", muk_h, identf)

    if exact:
        x_feat, xe_feat = build_feat(p, x, TPC, ident, want_err=True)
    else:
        x_feat = build_feat(p, x, TPC, ident)
        xe_feat = None

    halo = p.sb("halo", [128, NCH], F32)
    halo8 = p.sb("halo8", [NCH, 128], F32)
    p.dma(halo8[:], halo_in[0, :].rearrange("(c q) -> c q", q=128))
    pst0 = p.ps("halo_ps", [128, 16], F32)
    p.transp(pst0[:, 0:NCH], halo8[:], identf[0:NCH, 0:NCH])
    p.copy(halo[:], pst0[:, 0:NCH])

    mixb = p.sb("mixb", [128, NCH * HT], BF16)
    mixe = p.sb("mixe", [128, NCH * HT], BF16) if exact else None
    mixf = p.sb("mixf", [128, HT], F32)
    dif = p.sb("mix_dif", [128, HT], F32)
    om = p.sb("mix_om", [128, 1], F32)
    om2 = p.sb("mix_om2", [128, 1], F32)
    rT = p.sb("rT", [128, (1 if exact else NCH) * HT], F32 if exact else BF16)
    kab = p.sb("kab", [128, NCH * HT], BF16)
    kae = p.sb("kae", [128, NCH * HT], BF16) if exact else None
    relu_f = p.sb("relu_f", [128, HT], F32)
    wraw = p.sb("wraw", [128, D], BF16)
    wrawf = p.sb("wrawf", [128, D], F32) if exact else None
    wT_b = p.sb("wT_b", [128, (1 if exact else NCH) * D], BF16)
    wT_e = p.sb("wT_e", [128, D], BF16) if exact else None
    tbuf = p.ps("tbuf", [128, D], BF16)
    psA = p.ps("psA", [128, HT], F32)
    pstr = p.ps("pstr", [128, 512], F32)
    sH = [p.sb(f"sH{tt}", [128, D], F32) for tt in range(HT // 128)]

    xfull = p.sb("xfull", [128, HT + 1], F32) if exact else None

    def mix(mu, half):
        t0 = half * HT
        for c in range(NCH):
            muc = mu[:, c:c + 1]
            base = c * TPC + t0
            if exact:
                lo0 = 1 if t0 == 0 else 0
                p.tt(xfull[:, lo0:HT + 1],
                     x_feat[:, base + lo0 - 1:base + HT],
                     xe_feat[:, base + lo0 - 1:base + HT], ALU.add)
                if t0 == 0:
                    p.copy(xfull[:, 0:1], halo[:, c:c + 1])
                p.tt(dif[:], xfull[:, 1:HT + 1], xfull[:, 0:HT],
                     ALU.subtract)
                p.ts(dif[:], dif[:], muc, None, ALU.mult)
                p.tt(mixf[:], dif[:], xfull[:, 0:HT], ALU.add)
            else:
                if t0 == 0:
                    p.ts(om, halo[:, c:c + 1], muc, None, ALU.mult)
                    p.tt(om2, halo[:, c:c + 1], om, ALU.subtract)
                    p.stt(mixf[:, 0:1], x_feat[:, base:base + 1],
                          muc, om2, ALU.mult, ALU.add)
                    lo = 1
                else:
                    lo = 0
                p.tt(dif[:, lo:HT], x_feat[:, base + lo:base + HT],
                     x_feat[:, base + lo - 1:base + HT - 1], ALU.subtract)
                p.ts(dif[:, lo:HT], dif[:, lo:HT], muc, None, ALU.mult)
                p.tt(mixf[:, lo:HT], dif[:, lo:HT],
                     x_feat[:, base + lo - 1:base + HT - 1], ALU.add)
            p.copy(mixb[:, c * HT:(c + 1) * HT], mixf[:])
            if exact:
                p.tt(mixf[:], mixf[:], mixb[:, c * HT:(c + 1) * HT],
                     ALU.subtract)
                p.copy(mixe[:, c * HT:(c + 1) * HT], mixf[:])

    def load_wT_dd(W, dd):
        if exact:
            p.dma(wrawf[:], W[dd * 128:(dd + 1) * 128, :])
            p.copy(wraw[:], wrawf[:])
            for c in range(NCH):
                p.transp(tbuf[:, c * 128:(c + 1) * 128],
                         wraw[:, c * 128:(c + 1) * 128], ident)
            p.copy(wT_b[:, 0:D], tbuf[:])
            p.tt(wrawf[:], wrawf[:], wraw[:], ALU.subtract)
            p.copy(wraw[:], wrawf[:])
            for c in range(NCH):
                p.transp(tbuf[:, c * 128:(c + 1) * 128],
                         wraw[:, c * 128:(c + 1) * 128], ident)
            p.copy(wT_e[:, 0:D], tbuf[:])
        else:
            p.dma(wraw[:], W[dd * 128:(dd + 1) * 128, :], lane="poold")
            for c in range(NCH):
                p.transp(tbuf[:, c * 128:(c + 1) * 128],
                         wraw[:, c * 128:(c + 1) * 128], ident)
            p.copy(wT_b[:, dd * D:(dd + 1) * D], tbuf[:])

    def mm_dd(dd, srcb, srce):
        ddw = 0 if exact else dd
        for c in range(NCH):
            for j in range(HT // 512):
                sl = slice(j * 512, (j + 1) * 512)
                wb = wT_b[:, ddw * D + c * 128: ddw * D + (c + 1) * 128]
                p.mm(psA[:, sl], wb, srcb[:, c * HT + j * 512:
                                          c * HT + (j + 1) * 512],
                     start=(c == 0), stop=(not exact and c == NCH - 1))
                if exact:
                    we = wT_e[:, c * 128:(c + 1) * 128]
                    p.mm(psA[:, sl], wb, srce[:, c * HT + j * 512:
                                              c * HT + (j + 1) * 512],
                         start=False, stop=False)
                    p.mm(psA[:, sl], we, srcb[:, c * HT + j * 512:
                                              c * HT + (j + 1) * 512],
                         start=False, stop=(c == NCH - 1))

    for half in range(2):
        t0 = half * HT
        mix(muk, half)
        if not exact:
            for dd in range(NCH):
                load_wT_dd(Wk, dd)
        for dd in range(NCH):
            if exact:
                load_wT_dd(Wk, dd)
            mm_dd(dd, mixb, mixe)
            p.ts(relu_f[:], psA[:], 0.0, None, ALU.max)
            if exact:
                p.tt(relu_f[:], relu_f[:], relu_f[:], ALU.mult)
                p.copy(kab[:, dd * HT:(dd + 1) * HT], relu_f[:])
                p.tt(relu_f[:], relu_f[:], kab[:, dd * HT:(dd + 1) * HT],
                     ALU.subtract)
                p.copy(kae[:, dd * HT:(dd + 1) * HT], relu_f[:])
            else:
                p.act(kab[:, dd * HT:(dd + 1) * HT], relu_f[:], AF.Square)
        mix(mur, half)
        if not exact:
            for dd in range(NCH):
                load_wT_dd(Wr, dd)
            for dd in range(NCH):
                mm_dd(dd, mixb, mixe)
                p.act(rT[:, dd * HT:(dd + 1) * HT], psA[:], AF.Sigmoid)
            for dd in range(NCH):
                load_wT_dd(Wv, dd)
        for dd in range(NCH):
            if exact:
                load_wT_dd(Wr, dd)
                mm_dd(dd, mixb, mixe)
                p.act(rT[:, 0:HT], psA[:], AF.Sigmoid)
                load_wT_dd(Wv, dd)
            rslice = rT[:, 0:HT] if exact else rT[:, dd * HT:(dd + 1) * HT]
            mm_dd(dd, kab, kae)
            p.tt(relu_f[:], rslice, psA[:], ALU.mult)
            p.tt(relu_f[:], relu_f[:],
                 x_feat[:, dd * TPC + t0: dd * TPC + t0 + HT], ALU.add)
            if exact:
                p.tt(relu_f[:], relu_f[:],
                     xe_feat[:, dd * TPC + t0: dd * TPC + t0 + HT], ALU.add)
            for g in range(HT // 512):
                for j in range(4):
                    tt = g * 4 + j
                    p.transp(pstr[:, j * 128:(j + 1) * 128],
                             relu_f[:, tt * 128:(tt + 1) * 128], identf)
                for j in range(4):
                    tt = g * 4 + j
                    p.copy(sH[tt][:, dd * 128:(dd + 1) * 128],
                           pstr[:, j * 128:(j + 1) * 128])
        off = half * (HT // 128)
        layernorm_store(
            p, [(off + i, sH[i][:]) for i in range(HT // 128)],
            lambda tt: xo[tt * 128:(tt + 1) * 128, :])
    p.build()
    return nc, p


def chanmix_in_maps(x_full, Wr, Wk, Wv, mur, muk):
    maps = []
    for c in range(NCORES):
        b, h = c // 2, c % 2
        rows = x_full[b, h * TPC:(h + 1) * TPC, :]
        halo = (np.zeros((1, D), np.float32) if h == 0
                else x_full[b, TPC - 1:TPC, :])
        maps.append({
            "x_my": np.ascontiguousarray(rows),
            "halo_in": np.ascontiguousarray(halo),
            "W_r": Wr, "W_k": Wk, "W_v": Wv, "mu_r": mur, "mu_k": muk,
        })
    return maps


def gather_token_sharded(results, key="xo_my"):
    out = np.empty((B, T, D), np.float32)
    for c in range(NCORES):
        b, h = c // 2, c % 2
        out[b, h * TPC:(h + 1) * TPC, :] = results[c][key]
    return out


def new_nc():
    return bass.Bass("TRN2", target_bir_lowering=False)



def build_attnproj_nc(transposed, yfeat=False, exact=False):
    nc = new_nc()
    if yfeat:
        y = nc.dram_tensor("yT_my", [D, TPC], F32, kind="ExternalInput").ap()
    else:
        y = nc.dram_tensor("y_my", [TPC, D], F32, kind="ExternalInput").ap()
    x = nc.dram_tensor("x_my", [TPC, D], F32, kind="ExternalInput").ap()
    W = nc.dram_tensor("W", [D, D], F32, kind="ExternalInput").ap()
    bias = nc.dram_tensor("bias", [1, D], F32, kind="ExternalInput").ap()
    xo = nc.dram_tensor("xo_my", [TPC, D], F32, kind="ExternalOutput").ap()

    p = Prog(nc, "attnproj")
    ident, identf = load_ident(p)
    ye_feat = None
    if yfeat:
        y_feat = p.sb("y_feat", [128, NCH * TPC], BF16)
        if exact:
            ye_feat = p.sb("ye_feat", [128, NCH * TPC], BF16)
            yf32 = p.sb("yf32", [128, TPC], F32)
            for c in range(NCH):
                p.dma(yf32[:], y[c * 128:(c + 1) * 128, :])
                p.copy(y_feat[:, c * TPC:(c + 1) * TPC], yf32[:])
                p.tt(yf32[:], yf32[:], y_feat[:, c * TPC:(c + 1) * TPC],
                     ALU.subtract)
                p.copy(ye_feat[:, c * TPC:(c + 1) * TPC], yf32[:])
        else:
            for c in range(NCH):
                p.dma(y_feat[:, c * TPC:(c + 1) * TPC],
                      y[c * 128:(c + 1) * 128, :], lane="poold")
    else:
        assert not exact
        y_feat = build_feat(p, y, TPC, ident, name="y")
    ones1 = p.const_sb("ones1", np.ones((1, 128), np.float32), BF16)
    brow = p.sb("brow", [1, D], BF16)
    p.dma(brow[:], bias[:], lane="poold")
    psb = p.ps("psb", [128, D], F32)
    for j in range(2):
        p.mm(psb[:, j * 512:(j + 1) * 512], ones1[:],
             brow[:, j * 512:(j + 1) * 512])
    bias_bc = p.sb("bias_bc", [128, D], F32)
    p.copy(bias_bc[:], psb[:])

    wrhs = p.sb("wrhs", [128, NCH * D], BF16)
    wrhse = p.sb("wrhse", [128, NCH * D], BF16) if exact else None
    if not transposed:
        if exact:
            wf = p.sb("wf0", [128, D], F32)
            for c in range(NCH):
                p.dma(wf[:], W[c * 128:(c + 1) * 128, :])
                p.copy(wrhs[:, c * D:(c + 1) * D], wf[:])
                p.tt(wf[:], wf[:], wrhs[:, c * D:(c + 1) * D], ALU.subtract)
                p.copy(wrhse[:, c * D:(c + 1) * D], wf[:])
        else:
            for c in range(NCH):
                p.dma(wrhs[:, c * D:(c + 1) * D], W[c * 128:(c + 1) * 128, :],
                      lane="poold")
    else:
        wraw = p.sb("wraw", [128, D], BF16)
        tbuf = p.ps("tbuf", [128, D], BF16)
        wf = p.sb("wf0", [128, D], F32) if exact else None
        for dd in range(NCH):
            if exact:
                p.dma(wf[:], W[dd * 128:(dd + 1) * 128, :])
                p.copy(wraw[:], wf[:])
            else:
                p.dma(wraw[:], W[dd * 128:(dd + 1) * 128, :], lane="poold")
            for c in range(NCH):
                p.transp(tbuf[:, c * 128:(c + 1) * 128],
                         wraw[:, c * 128:(c + 1) * 128], ident)
            for c in range(NCH):
                p.copy(wrhs[:, c * D + dd * 128: c * D + (dd + 1) * 128],
                       tbuf[:, c * 128:(c + 1) * 128])
            if exact:
                p.tt(wf[:], wf[:], wraw[:], ALU.subtract)
                p.copy(wraw[:], wf[:])
                for c in range(NCH):
                    p.transp(tbuf[:, c * 128:(c + 1) * 128],
                             wraw[:, c * 128:(c + 1) * 128], ident)
                for c in range(NCH):
                    p.copy(wrhse[:, c * D + dd * 128: c * D + (dd + 1) * 128],
                           tbuf[:, c * 128:(c + 1) * 128])

    x_tok = p.sb("x_tok", [128, D], F32)
    s_t = p.sb("s_t", [128, D], F32)
    psA = p.ps("psA", [128, D], F32)
    for tt in range(NTT):
        p.dma(x_tok[:], x[tt * 128:(tt + 1) * 128, :])
        for j in range(2):
            for c in range(NCH):
                p.mm(psA[:, j * 512:(j + 1) * 512],
                     y_feat[:, c * TPC + tt * 128: c * TPC + (tt + 1) * 128],
                     wrhs[:, c * D + j * 512: c * D + (j + 1) * 512],
                     start=(c == 0), stop=(not exact and c == NCH - 1))
                if exact:
                    p.mm(psA[:, j * 512:(j + 1) * 512],
                         ye_feat[:, c * TPC + tt * 128:
                                 c * TPC + (tt + 1) * 128],
                         wrhs[:, c * D + j * 512: c * D + (j + 1) * 512],
                         start=False, stop=False)
                    p.mm(psA[:, j * 512:(j + 1) * 512],
                         y_feat[:, c * TPC + tt * 128:
                                c * TPC + (tt + 1) * 128],
                         wrhse[:, c * D + j * 512: c * D + (j + 1) * 512],
                         start=False, stop=(c == NCH - 1))
        p.tt(s_t[:], x_tok[:], bias_bc[:], ALU.add)
        p.tt(s_t[:], s_t[:], psA[:], ALU.add)
        layernorm_store(p, [(tt, s_t[:])],
                        lambda q: xo[q * 128:(q + 1) * 128, :])
    p.build()
    return nc, p


def attnproj_in_maps(y_full, x_full, W, bias):
    maps = []
    for c in range(NCORES):
        b, h = c // 2, c % 2
        maps.append({
            "y_my": np.ascontiguousarray(y_full[b, h * TPC:(h + 1) * TPC, :]),
            "x_my": np.ascontiguousarray(x_full[b, h * TPC:(h + 1) * TPC, :]),
            "W": W, "bias": np.ascontiguousarray(bias.reshape(1, D)),
        })
    return maps



def build_l1ffn_nc():
    nc = new_nc()
    NE = 8
    x = nc.dram_tensor("x_my", [TPC, D], F32, kind="ExternalInput").ap()
    sigs = nc.dram_tensor("sigs", [NE, D], F32, kind="ExternalInput").ap()
    W = nc.dram_tensor("W", [NE, D, D], F32, kind="ExternalInput").ap()
    b8 = nc.dram_tensor("bias8", [NE, D], F32, kind="ExternalInput").ap()
    xo = nc.dram_tensor("xo_my", [TPC, D], F32, kind="ExternalOutput").ap()

    p = Prog(nc, "l1ffn")
    ident, identf = load_ident(p)
    x_feat, xe_feat = build_feat(p, x, TPC, ident, want_err=True)

    sigraw = p.sb("sigraw", [NE, D], F32)
    p.dma(sigraw[:], sigs[:])
    sigsign = p.sb("sigsign", [NE, D], BF16)
    p.act(sigsign[:], sigraw[:], AF.Sign)
    signT = p.sb("signT", [128, NCH * NE], BF16)
    pst8 = p.ps("sig_ps", [128, 16], BF16)
    for c in range(NCH):
        p.transp(pst8[:, 0:NE], sigsign[:, c * 128:(c + 1) * 128],
                 ident[0:NE, 0:NE])
        p.copy(signT[:, c * NE:(c + 1) * NE], pst8[:, 0:NE])

    bias_all = p.sb("bias_all", [NE, D], BF16)
    p.dma(bias_all[:], b8[:, :], lane="poold")

    psS = p.ps("psS", [128, 8], F32)
    sc = p.sb("sc", [128, NE], F32)
    t8 = p.sb("t8", [128, 8], F32)
    i8 = p.sb("i8", [128, 8], U16)
    idxf = p.sb("idxf", [128, 1], F32)
    onehot = p.sb("onehot", [128, NE], BF16)
    psT = p.ps("psT", [NE, 128], BF16)
    ohT = p.sb("ohT", [NE, NTT * 128], BF16)
    psB = p.ps("psB", [128, D], F32)
    psO = p.ps("psO", [128, 512], F32)
    acc = [p.sb(f"acc{tt}", [128, D], F32) for tt in range(NTT)]
    masks = p.sb("masks", [128, NTT * NE], F32)

    for tt in range(NTT):
        for c in range(NCH):
            p.mm(psS[:], x_feat[:, c * TPC + tt * 128: c * TPC + (tt + 1) * 128],
                 signT[:, c * NE:(c + 1) * NE],
                 start=(c == 0), stop=False)
        for c in range(NCH):
            p.mm(psS[:], xe_feat[:, c * TPC + tt * 128: c * TPC + (tt + 1) * 128],
                 signT[:, c * NE:(c + 1) * NE],
                 start=False, stop=(c == NCH - 1))
        p.copy(sc[:], psS[:])
        p.vmax8(t8[:], sc[:])
        p.vmaxidx(i8[:], t8[:], sc[:])
        p.copy(idxf[:], i8[:, 0:1])
        for e in range(NE):
            p.ts(masks[:, tt * NE + e: tt * NE + e + 1], idxf[:], float(e),
                 None, ALU.is_equal)
        p.copy(onehot[:], masks[:, tt * NE:(tt + 1) * NE])
        p.transp(psT[:], onehot[:], ident)
        p.copy(ohT[:, tt * 128:(tt + 1) * 128], psT[:])
        for j in range(2):
            p.mm(psB[:, j * 512:(j + 1) * 512], ohT[:, tt * 128:(tt + 1) * 128],
                 bias_all[:, j * 512:(j + 1) * 512])
        p.copy(acc[tt][:], psB[:])

    we = p.sb("we", [128, NCH * D], BF16)
    for e in range(NE):
        for c in range(NCH):
            p.dma(we[:, c * D:(c + 1) * D], W[e, c * 128:(c + 1) * 128, :],
                  lane="poold")
        for tt in range(NTT):
            for j in range(2):
                for c in range(NCH):
                    p.mm(psO[:],
                         x_feat[:, c * TPC + tt * 128: c * TPC + (tt + 1) * 128],
                         we[:, c * D + j * 512: c * D + (j + 1) * 512],
                         start=(c == 0), stop=(c == NCH - 1))
                p.stt(acc[tt][:, j * 512:(j + 1) * 512], psO[:],
                      masks[:, tt * NE + e: tt * NE + e + 1],
                      acc[tt][:, j * 512:(j + 1) * 512], ALU.mult, ALU.add)

    x_tok = p.sb("x_tok", [128, D], F32)
    s_t = p.sb("s_t", [128, D], F32)
    for tt in range(NTT):
        p.dma(x_tok[:], x[tt * 128:(tt + 1) * 128, :])
        p.tt(s_t[:], x_tok[:], acc[tt][:], ALU.add)
        layernorm_store(p, [(tt, s_t[:])],
                        lambda q: xo[q * 128:(q + 1) * 128, :])
    p.build()
    return nc, p


def tsharded_in_maps(x_full, extra):
    maps = []
    for c in range(NCORES):
        b, h = c // 2, c % 2
        m = {"x_my": np.ascontiguousarray(x_full[b, h * TPC:(h + 1) * TPC, :])}
        m.update(extra)
        maps.append(m)
    return maps



def _rope_tables():
    d = 64
    inv = 1.0 / (10000.0 ** (np.arange(0, d, 2, dtype=np.float64) / d))
    t = np.arange(T, dtype=np.float64)
    fr = np.outer(t, inv)
    return np.cos(fr).astype(np.float32), np.sin(fr).astype(np.float32)


def causal_attn(p, ident, head_ops, y, scale, name="ca"):
    NT = T // 128
    s_row = p.sb(name + "_s_row", [128, T], F32)
    e_row = p.sb(name + "_e_row", [128, T], BF16)
    eT = p.sb(name + "_eT", [128, T], BF16)
    rowstat = p.sb(name + "_rowstat", [128, 4], F32)
    psS = p.ps(name + "_psS", [128, 512], F32)
    psAV = p.ps(name + "_psAV", [128, 64], F32)
    psTr = p.ps("psTr_sh", [128, 1024], BF16)
    ytile = p.sb(name + "_ytile", [128, 64], F32)
    mask_np = np.where(np.arange(128)[:, None] >= np.arange(128)[None, :],
                       0.0, -1e30).astype(np.float32)
    cmask = p.const_sb(name + "_cmask", mask_np, F32)
    for ho in head_ops:
        if len(ho) == 5:
            qrows_fn, krows, v_fn, ycol, pre_fn = ho
            pre_fn()
        else:
            qrows_fn, krows, v_fn, ycol = ho
        for qt in range(NT):
            nk = (qt + 1) * 128
            qr_ap = qrows_fn(qt)
            for kg in range((nk + 511) // 512):
                k0, k1 = kg * 512, min((kg + 1) * 512, nk)
                p.mm(psS[:, 0:k1 - k0], qr_ap, krows[:, k0:k1])
                p.copy(s_row[:, k0:k1], psS[:, 0:k1 - k0])
            p.tt(s_row[:, nk - 128:nk], s_row[:, nk - 128:nk], cmask[:],
                 ALU.add)
            p.red(rowstat[:, 0:1], s_row[:, 0:nk], ALU.max)
            p.ts(rowstat[:, 1:2], rowstat[:, 0:1], -scale, None, ALU.mult)
            p.act(e_row[:, 0:nk], s_row[:, 0:nk], AF.Exp,
                  bias=rowstat[:, 1:2], scale=scale,
                  accum_out=rowstat[:, 2:3])
            p.recip(rowstat[:, 3:4], rowstat[:, 2:3])
            nt_k = nk // 128
            for g8 in range((nt_k + 7) // 8):
                j0 = g8 * 8
                j1 = min(j0 + 8, nt_k)
                for j in range(j0, j1):
                    p.transp(psTr[:, (j - j0) * 128:(j - j0 + 1) * 128],
                             e_row[:, j * 128:(j + 1) * 128], ident)
                p.copy(eT[:, j0 * 128:j1 * 128],
                       psTr[:, 0:(j1 - j0) * 128])
            for j in range(nt_k):
                p.mm(psAV[:], eT[:, j * 128:(j + 1) * 128], v_fn(j),
                     start=(j == 0), stop=(j == nt_k - 1))
            p.ts(ytile[:], psAV[:], rowstat[:, 3:4], None, ALU.mult)
            p.dma_out(y[qt * 128:(qt + 1) * 128, ycol:ycol + 64],
                      ytile[:], lane="sp")


def build_l1attn_nc():
    nc = new_nc()
    NH, NKV, HD = 8, 2, 64
    x = nc.dram_tensor("x_all", [T, D], F32, kind="ExternalInput").ap()
    Wqa = nc.dram_tensor("Wqa", [D, 64], F32, kind="ExternalInput").ap()
    Wqb = nc.dram_tensor("Wqb", [64, 512], F32, kind="ExternalInput").ap()
    Wkva = nc.dram_tensor("Wkva", [D, 32], F32, kind="ExternalInput").ap()
    Wkvb = nc.dram_tensor("Wkvb", [32, 256], F32, kind="ExternalInput").ap()
    y = nc.dram_tensor("y_my", [T, 512], F32, kind="ExternalOutput").ap()

    p = Prog(nc, "l1attn")
    ident, identf = load_ident(p)
    NT = T // 128
    x_feat = build_feat(p, x, T, ident)

    cos_np, sin_np = _rope_tables()
    cos_d = p.const_dram(np.ascontiguousarray(cos_np))
    sin_d = p.const_dram(np.ascontiguousarray(sin_np))
    cos_all = p.sb("cos_all", [128, NT * 32], F32)
    sin_all = p.sb("sin_all", [128, NT * 32], F32)
    p.dma(cos_all[:].rearrange("q (n i) -> q n i", i=32),
          cos_d.ap().rearrange("(n q) i -> q n i", q=128))
    p.dma(sin_all[:].rearrange("q (n i) -> q n i", i=32),
          sin_d.ap().rearrange("(n q) i -> q n i", q=128))

    wqa_sb = p.sb("wqa_sb", [128, NCH * 64], BF16)
    wkva_sb = p.sb("wkva_sb", [128, NCH * 32], BF16)
    for c in range(NCH):
        p.dma(wqa_sb[:, c * 64:(c + 1) * 64], Wqa[c * 128:(c + 1) * 128, :],
              lane="poold")
        p.dma(wkva_sb[:, c * 32:(c + 1) * 32], Wkva[c * 128:(c + 1) * 128, :],
              lane="poold")
    t1T = p.sb("t1T", [64, T], BF16)
    t2T = p.sb("t2T", [32, T], BF16)
    psL = p.ps("psL", [64, 512], F32)
    psL2 = p.ps("psL2", [32, 512], F32)
    for j in range(T // 512):
        for c in range(NCH):
            p.mm(psL[:], wqa_sb[:, c * 64:(c + 1) * 64],
                 x_feat[:, c * T + j * 512: c * T + (j + 1) * 512],
                 start=(c == 0), stop=(c == NCH - 1))
        p.copy(t1T[:, j * 512:(j + 1) * 512], psL[:])
        for c in range(NCH):
            p.mm(psL2[:], wkva_sb[:, c * 32:(c + 1) * 32],
                 x_feat[:, c * T + j * 512: c * T + (j + 1) * 512],
                 start=(c == 0), stop=(c == NCH - 1))
        p.copy(t2T[:, j * 512:(j + 1) * 512], psL2[:])

    wqb_sb = p.sb("wqb_sb", [64, 512], BF16)
    wkvb_sb = p.sb("wkvb_sb", [32, 256], BF16)
    p.dma(wqb_sb[:], Wqb[:], lane="poold")
    p.dma(wkvb_sb[:], Wkvb[:], lane="poold")

    q_feat = p.sb("q_feat", [128, 4 * T], BF16)
    k_feat = p.sb("k_feat", [128, T], BF16)
    k_feat_sw = p.sb("k_feat_sw", [128, T], BF16)
    v_tok = p.sb("v_tok", [128, T], BF16)
    q_tok = p.sb("q_tok", [128, 512], F32)
    kv_tok = p.sb("kv_tok", [128, 256], F32)
    qr = p.sb("qr", [128, 512], BF16)
    kr = p.sb("kr", [128, 128], BF16)
    rt1 = p.sb("rt1", [128, 256], F32)
    rt2 = p.sb("rt2", [128, 256], F32)
    psQ = p.ps("psQ", [128, 512], F32)
    psKV = p.ps("psKV", [128, 256], F32)
    psTr = p.ps("psTr_sh", [128, 1024], BF16)

    for tt in range(NT):
        cos_t = cos_all[:, tt * 32:(tt + 1) * 32]
        sin_t = sin_all[:, tt * 32:(tt + 1) * 32]
        p.mm(psQ[:], t1T[:, tt * 128:(tt + 1) * 128], wqb_sb[:])
        p.copy(q_tok[:], psQ[:])
        p.mm(psKV[:], t2T[:, tt * 128:(tt + 1) * 128], wkvb_sb[:])
        p.copy(kv_tok[:], psKV[:])
        qe = q_tok[:].rearrange("p (h i t) -> p h i t", i=32, t=2)
        c8 = cos_t.rearrange("p (o i) -> p o i", o=1).to_broadcast([128, 8, 32])
        s8 = sin_t.rearrange("p (o i) -> p o i", o=1).to_broadcast([128, 8, 32])
        r1a = rt1[:].rearrange("p (h i) -> p h i", i=32)
        r2a = rt2[:].rearrange("p (h i) -> p h i", i=32)
        qre = qr[:].rearrange("p (h i t) -> p h i t", i=32, t=2)
        p.tt(r1a, qe[:, :, :, 0], c8, ALU.mult)
        p.tt(r2a, qe[:, :, :, 1], s8, ALU.mult)
        p.tt(qre[:, :, :, 0], r1a, r2a, ALU.subtract)
        p.tt(r1a, qe[:, :, :, 0], s8, ALU.mult)
        p.tt(r2a, qe[:, :, :, 1], c8, ALU.mult)
        p.tt(qre[:, :, :, 1], r1a, r2a, ALU.add)
        kve = kv_tok[:].rearrange("p (kvh i t comp) -> p kvh i t comp",
                                  i=32, t=2, comp=2)
        c2 = cos_t.rearrange("p (o i) -> p o i", o=1).to_broadcast([128, 2, 32])
        s2 = sin_t.rearrange("p (o i) -> p o i", o=1).to_broadcast([128, 2, 32])
        k1a = rt1[:, 0:64].rearrange("p (h i) -> p h i", i=32)
        k2a = rt2[:, 0:64].rearrange("p (h i) -> p h i", i=32)
        kre = kr[:].rearrange("p (kvh i t) -> p kvh i t", i=32, t=2)
        p.tt(k1a, kve[:, :, :, 0, 0], c2, ALU.mult)
        p.tt(k2a, kve[:, :, :, 1, 0], s2, ALU.mult)
        p.tt(kre[:, :, :, 0], k1a, k2a, ALU.subtract)
        p.tt(k1a, kve[:, :, :, 0, 0], s2, ALU.mult)
        p.tt(k2a, kve[:, :, :, 1, 0], c2, ALU.mult)
        p.tt(kre[:, :, :, 1], k1a, k2a, ALU.add)
        vv = kv_tok[:].rearrange("p (kvh d comp) -> p kvh d comp", d=64, comp=2)
        vd = v_tok[:, tt * 128:(tt + 1) * 128].rearrange(
            "p (kvh d) -> p kvh d", d=64)
        p.copy(vd, vv[:, :, :, 1])
        for qc in range(4):
            p.transp(psTr[:, qc * 128:(qc + 1) * 128],
                     qr[:, qc * 128:(qc + 1) * 128], ident)
        p.transp(psTr[:, 512:640], kr[:], ident)
        for qc in range(4):
            p.copy(q_feat[:, qc * T + tt * 128: qc * T + (tt + 1) * 128],
                   psTr[:, qc * 128:(qc + 1) * 128])
        p.copy(k_feat[:, tt * 128:(tt + 1) * 128], psTr[:, 512:640])
        p.copy(k_feat_sw[0:64, tt * 128:(tt + 1) * 128],
               psTr[64:128, 512:640])
        p.copy(k_feat_sw[64:128, tt * 128:(tt + 1) * 128],
               psTr[0:64, 512:640])

    SCALE = 64.0 ** -0.5
    head_ops = []
    for h in range(NH):
        kvh = h // 4
        qc, sub = h // 2, h % 2
        ksrc = k_feat if kvh == sub else k_feat_sw
        krows = ksrc[sub * 64:(sub + 1) * 64, :]
        head_ops.append((
            (lambda qt, qc=qc, sub=sub: q_feat[sub * 64:(sub + 1) * 64,
                                               qc * T + qt * 128:
                                               qc * T + (qt + 1) * 128]),
            krows,
            (lambda kt, kvh=kvh: v_tok[:, kt * 128 + kvh * 64:
                                       kt * 128 + (kvh + 1) * 64]),
            h * 64))
    causal_attn(p, ident, head_ops, y, SCALE)
    p.build()
    return nc, p


def l1attn_in_maps(x_full, Wqa, Wqb, Wkva, Wkvb):
    maps = []
    for c in range(NCORES):
        b, g = c // 2, c % 2
        maps.append({
            "x_all": np.ascontiguousarray(x_full[b]),
            "Wqa": Wqa,
            "Wqb": np.ascontiguousarray(Wqb[:, g * 512:(g + 1) * 512]),
            "Wkva": Wkva,
            "Wkvb": np.ascontiguousarray(Wkvb[:, g * 256:(g + 1) * 256]),
        })
    return maps


def gather_head_sharded(results, key="y_my", width=512):
    out = np.empty((B, T, 2 * width), np.float32)
    for c in range(NCORES):
        b, g = c // 2, c % 2
        out[b, :, g * width:(g + 1) * width] = results[c][key]
    return out



def build_l2attn_nc():
    nc = new_nc()
    NH, R = 8, 64
    x = nc.dram_tensor("x_all", [T, D], F32, kind="ExternalInput").ap()
    basis = nc.dram_tensor("basis_w", [R, D], F32, kind="ExternalInput").ap()
    core64 = nc.dram_tensor("core64", [R, R], F32, kind="ExternalInput").ap()
    hres = nc.dram_tensor("head_res", [16, R, R], F32, kind="ExternalInput").ap()
    hres_my = nc.dram_tensor("hres_my", [NH, R, R], F32, kind="ExternalInput").ap()
    v_w = nc.dram_tensor("v_w512", [512, D], F32, kind="ExternalInput").ap()
    v_b = nc.dram_tensor("v_b512", [1, 512], F32, kind="ExternalInput").ap()
    y = nc.dram_tensor("y_my", [T, 512], F32, kind="ExternalOutput").ap()

    p = Prog(nc, "l2attn")
    ident, identf = load_ident(p)
    NT = T // 128
    x_feat = build_feat(p, x, T, ident)

    bT = p.sb("bT", [128, NCH * R], BF16)
    braw = p.sb("braw", [R, D], BF16)
    p.dma(braw[:], basis[:], lane="poold")
    psTrS = p.ps("psTr_sh", [128, 1024], BF16)
    psb8 = psTrS
    for c in range(NCH):
        p.transp(psb8[:, 0:R], braw[:, c * 128:(c + 1) * 128], ident[0:R, 0:R])
        p.copy(bT[:, c * R:(c + 1) * R], psb8[:, 0:R])
    latT = p.sb("latT", [R, T], BF16)
    psL = p.ps("psL", [R, 512], F32)
    for j in range(T // 512):
        for c in range(NCH):
            p.mm(psL[:], bT[:, c * R:(c + 1) * R],
                 x_feat[:, c * T + j * 512: c * T + (j + 1) * 512],
                 start=(c == 0), stop=(c == NCH - 1))
        p.copy(latT[:, j * 512:(j + 1) * 512], psL[:])

    craw = p.sb("craw", [R, R], F32)
    p.dma(craw[:], core64[:])
    psF = p.ps("psF_sh", [128, 512], F32)
    psc = psF
    p.transp(psc[0:R, 0:R], craw[:], identf[0:R, 0:R])
    csym = p.sb("csym", [R, R], F32)
    p.tt(csym[:], craw[:], psc[0:R, 0:R], ALU.add)
    p.ts(csym[:], csym[:], 0.5 / 16.0, None, ALU.mult)
    hall = p.sb("hall", [R, 16 * R], F32)
    p.dma(hall[:].rearrange("r (g q) -> r g q", q=R),
          hres.rearrange("g r q -> r g q"))
    hmy = p.sb("hmy", [R, NH * R], F32)
    p.dma(hmy[:].rearrange("r (g q) -> r g q", q=R),
          hres_my.rearrange("g r q -> r g q"))
    hmean = p.sb("hmean", [R, R], F32)
    hm2 = p.sb("hm2", [R, R], F32)
    p.tt(hmean[:], hall[:, 0:R], hall[:, R:2 * R], ALU.add)
    for g in range(2, 16):
        src = hmean if g % 2 == 0 else hm2
        dst = hm2 if g % 2 == 0 else hmean
        p.tt(dst[:], src[:], hall[:, g * R:(g + 1) * R], ALU.add)
    p.ts(hmean[:], hmean[:], 1.0 / 16.0, None, ALU.mult)
    p.tt(csym[:], csym[:], hmean[:], ALU.subtract)
    HM = p.sb("HM", [R, NH * R], BF16)
    for j in range(NH):
        p.tt(HM[:, j * R:(j + 1) * R], csym[:], hmy[:, j * R:(j + 1) * R],
             ALU.add)

    ltT = p.sb("ltT", [R, T], BF16)

    def make_lt(j):
        for t5 in range(T // 512):
            p.mm(psL[:], HM[:, j * R:(j + 1) * R],
                 latT[:, t5 * 512:(t5 + 1) * 512])
            p.copy(ltT[:, t5 * 512:(t5 + 1) * 512], psL[:])

    vwT = p.sb("vwT", [128, NCH * 512], BF16)
    vraw = p.sb("vraw", [128, D], BF16)
    psTr0 = psTrS
    for jc in range(4):
        p.dma(vraw[:], v_w[jc * 128:(jc + 1) * 128, :], lane="poold")
        for c in range(NCH):
            p.transp(psTr0[:, c * 128:(c + 1) * 128],
                     vraw[:, c * 128:(c + 1) * 128], ident)
        for c in range(NCH):
            p.copy(vwT[:, c * 512 + jc * 128: c * 512 + (jc + 1) * 128],
                   psTr0[:, c * 128:(c + 1) * 128])
    ones1 = p.const_sb("ones1", np.ones((1, 128), np.float32), BF16)
    vbrow = p.sb("vbrow", [1, 512], BF16)
    p.dma(vbrow[:], v_b[:], lane="poold")
    p.mm(psF[:], ones1[:], vbrow[:])
    vb_bc = p.sb("vb_bc", [128, 512], F32)
    p.copy(vb_bc[:], psF[:])
    v_tok = p.sb("v_tok", [128, NT * 512], BF16)
    psV = psF
    for kt in range(NT):
        for c in range(NCH):
            p.mm(psV[:], x_feat[:, c * T + kt * 128: c * T + (kt + 1) * 128],
                 vwT[:, c * 512:(c + 1) * 512],
                 start=(c == 0), stop=(c == NCH - 1))
        p.tt(v_tok[:, kt * 512:(kt + 1) * 512], psV[:], vb_bc[:], ALU.add)

    head_ops = []
    for j in range(NH):
        head_ops.append((
            (lambda qt: ltT[0:R, qt * 128:(qt + 1) * 128]),
            latT[:],
            (lambda kt, j=j: v_tok[:, kt * 512 + j * 64: kt * 512 + (j + 1) * 64]),
            j * 64,
            (lambda j=j: make_lt(j))))
    causal_attn(p, ident, head_ops, y, 0.125)
    p.build()
    return nc, p


def l2attn_in_maps(x_full, basis_w, core64, head_res, v_w, v_b):
    maps = []
    for c in range(NCORES):
        b, g = c // 2, c % 2
        maps.append({
            "x_all": np.ascontiguousarray(x_full[b]),
            "basis_w": basis_w, "core64": core64, "head_res": head_res,
            "hres_my": np.ascontiguousarray(head_res[g * 8:(g + 1) * 8]),
            "v_w512": np.ascontiguousarray(v_w[g * 512:(g + 1) * 512, :]),
            "v_b512": np.ascontiguousarray(v_b[g * 512:(g + 1) * 512]
                                           .reshape(1, 512)),
        })
    return maps

def build_l0attn_nc():
    nc = new_nc()
    NH = 4
    x = nc.dram_tensor("x_all", [T, D], F32, kind="ExternalInput").ap()
    Wq = nc.dram_tensor("Wq512", [512, D], F32, kind="ExternalInput").ap()
    Wk = nc.dram_tensor("Wk512", [512, D], F32, kind="ExternalInput").ap()
    Wv = nc.dram_tensor("Wv512", [512, D], F32, kind="ExternalInput").ap()
    lq1 = nc.dram_tensor("lam_q1", [1, 64], F32, kind="ExternalInput").ap()
    lk1 = nc.dram_tensor("lam_k1", [1, 64], F32, kind="ExternalInput").ap()
    lq2 = nc.dram_tensor("lam_q2", [1, 64], F32, kind="ExternalInput").ap()
    lk2 = nc.dram_tensor("lam_k2", [1, 64], F32, kind="ExternalInput").ap()
    dgain = nc.dram_tensor("delta_gain4", [1, 4], F32, kind="ExternalInput").ap()
    chd = nc.dram_tensor("cos_head_delta", [1, 8], F32, kind="ExternalInput").ap()
    chdmy = nc.dram_tensor("chd_my", [1, 4], F32, kind="ExternalInput").ap()
    craw = nc.dram_tensor("cos_raw", [1, 1], F32, kind="ExternalInput").ap()
    subw = nc.dram_tensor("subln_w", [128, 1], F32, kind="ExternalInput").ap()
    yT = nc.dram_tensor("yT_my", [512, T], F32, kind="ExternalOutput").ap()

    qnb_d = nc.dram_tensor("qnb_d", [512, T], BF16).ap()
    qne_d = nc.dram_tensor("qne_d", [512, T], BF16).ap()
    knb_d = nc.dram_tensor("knb_d", [512, T], BF16).ap()
    kne_d = nc.dram_tensor("kne_d", [512, T], BF16).ap()
    vb_d = nc.dram_tensor("vb_d", [T, 512], BF16).ap()
    ve_d = nc.dram_tensor("ve_d", [T, 512], BF16).ap()

    p = Prog(nc, "l0attn")
    ident, identf = load_ident(p)
    NT = T // 128
    x_feat, xe_feat = build_feat(p, x, T, ident, want_err=True)
    psTrS = p.ps("psTr_sh", [128, 1024], BF16)
    psF = p.ps("psF_sh", [128, 512], F32)

    sc = p.sb("scal", [1, 16], F32)
    p.dma(sc[0:1, 0:1], craw[:])
    p.dma(sc[0:1, 1:9], chd[:])
    p.dma(sc[0:1, 9:13], chdmy[:])
    lrow = p.sb("lrow", [1, 256], F32)
    p.dma(lrow[0:1, 0:64], lq1[:])
    p.dma(lrow[0:1, 64:128], lk1[:])
    p.dma(lrow[0:1, 128:192], lq2[:])
    p.dma(lrow[0:1, 192:256], lk2[:])
    st = p.sb("st", [1, 32], F32)
    p.act(st[0:1, 0:1], sc[0:1, 0:1], AF.Sigmoid)
    p.red(st[0:1, 1:2], sc[0:1, 1:9], ALU.add)
    p.ts(st[0:1, 1:2], st[0:1, 1:2], 1.0 / 8.0, None, ALU.mult)
    p.ts(st[0:1, 2:6], sc[0:1, 9:13], st[0:1, 1:2], None, ALU.subtract)
    p.act(st[0:1, 6:10], st[0:1, 2:6], AF.Tanh)
    p.ts(st[0:1, 10:14], st[0:1, 6:10], 0.5, 1.0, ALU.mult, ALU.add)
    p.ts(st[0:1, 14:15], st[0:1, 0:1], 15.0, None, ALU.mult)
    p.ts(st[0:1, 10:14], st[0:1, 10:14], st[0:1, 14:15], None, ALU.mult)
    prod = p.sb("prodl", [1, 64], F32)
    lam = p.sb("lam", [1, 8], F32)
    p.tt(prod[:], lrow[0:1, 0:64], lrow[0:1, 64:128], ALU.mult)
    p.red(lam[0:1, 0:1], prod[:], ALU.add)
    p.tt(prod[:], lrow[0:1, 128:192], lrow[0:1, 192:256], ALU.mult)
    p.red(lam[0:1, 1:2], prod[:], ALU.add)
    p.act(lam[0:1, 2:3], lam[0:1, 0:1], AF.Exp)
    p.act(lam[0:1, 3:4], lam[0:1, 1:2], AF.Exp)
    p.tt(lam[0:1, 4:5], lam[0:1, 2:3], lam[0:1, 3:4], ALU.subtract)
    p.ts(lam[0:1, 5:6], lam[0:1, 4:5], 1.0, 0.5, ALU.mult, ALU.add)
    p.ts(lam[0:1, 6:7], lam[0:1, 5:6], -1.0, None, ALU.mult)
    ones1 = p.const_sb("ones1f", np.ones((1, 128), np.float32), BF16)
    strow = p.sb("strow", [1, 16], F32)
    p.copy(strow[0:1, 0:4], st[0:1, 10:14], lane="dve")
    p.copy(strow[0:1, 4:5], lam[0:1, 6:7], lane="dve")
    dgrow = p.sb("dgrow", [1, 4], F32)
    p.dma(dgrow[:], dgain[:])
    p.copy(strow[0:1, 5:9], dgrow[:], lane="dve")
    p.memset(strow[0:1, 9:16], 0.0)
    hi_sc = p.sb("hi_sc", [1, 64], BF16)
    lo_sc = p.sb("lo_sc", [1, 64], BF16)
    colset = p.sb("colset", [128, 16], F32)
    bcast_rows_f32(p, ones1, psF[:, 0:16], strow[0:1, 0:16], hi_sc, lo_sc,
                   colset[:], 16)
    subw_sb = p.sb("subw_sb", [128, 1], F32)
    p.dma(subw_sb[:], subw[:])
    p.ts(subw_sb[:], subw_sb[:], 0.5, None, ALU.mult)
    import os as _os2
    if _os2.environ.get("L0_DBGCOL"):
        dbgcol = nc.dram_tensor("dbgcol", [128, 16], F32,
                                kind="ExternalOutput").ap()
        p.dma_out(dbgcol[:], colset[:])

    wfull = p.sb("wfull", [128, D], F32)
    wb_t = p.sb("wb_t", [128, NCH * 512], BF16)
    we_t = p.sb("we_t", [128, NCH * 512], BF16)
    wbb = p.sb("wbb", [128, D], BF16)

    def load_wT_hl(W):
        for jc in range(4):
            p.dma(wfull[:], W[jc * 128:(jc + 1) * 128, :])
            p.copy(wbb[:], wfull[:])
            for c in range(NCH):
                p.transp(psTrS[:, c * 128:(c + 1) * 128],
                         wbb[:, c * 128:(c + 1) * 128], ident)
            for c in range(NCH):
                p.copy(wb_t[:, c * 512 + jc * 128: c * 512 + (jc + 1) * 128],
                       psTrS[:, c * 128:(c + 1) * 128])
            p.tt(wfull[:], wfull[:], wbb[:], ALU.subtract)
            p.copy(wbb[:], wfull[:])
            for c in range(NCH):
                p.transp(psTrS[:, c * 128:(c + 1) * 128],
                         wbb[:, c * 128:(c + 1) * 128], ident)
            for c in range(NCH):
                p.copy(we_t[:, c * 512 + jc * 128: c * 512 + (jc + 1) * 128],
                       psTrS[:, c * 128:(c + 1) * 128])

    ptok = p.sb("ptok", [128, 512], F32)
    ptb = p.sb("ptb", [128, 512], BF16)
    ptt = p.sb("ptt", [128, 512], BF16)
    sqn = p.sb("sqn", [128, 512], F32)
    nrm = p.sb("nrm", [128, 8], F32)
    COS_EPS = 1e-6

    def proj3(tt):
        for c in range(NCH):
            wb = wb_t[:, c * 512:(c + 1) * 512]
            we = we_t[:, c * 512:(c + 1) * 512]
            xb = x_feat[:, c * T + tt * 128: c * T + (tt + 1) * 128]
            xe = xe_feat[:, c * T + tt * 128: c * T + (tt + 1) * 128]
            p.mm(psF[:], xb, wb, start=(c == 0), stop=False)
            p.mm(psF[:], xe, wb, start=False, stop=False)
            p.mm(psF[:], xb, we, start=False, stop=(c == NCH - 1))
        p.copy(ptok[:], psF[:])

    def norm_cos(cos_fold, negate_half2):
        p.tt(sqn[:], ptok[:], ptok[:], ALU.mult)
        for j in range(NH):
            for h2 in range(2):
                c0 = j * 128 + h2 * 64
                p.red(nrm[:, 0:1], sqn[:, c0:c0 + 64], ALU.add)
                p.act(nrm[:, 1:2], nrm[:, 0:1], AF.Sqrt)
                p.recip(nrm[:, 4:5], nrm[:, 1:2])
                p.tt(nrm[:, 5:6], nrm[:, 0:1], nrm[:, 4:5], ALU.mult)
                p.tt(nrm[:, 6:7], nrm[:, 1:2], nrm[:, 5:6], ALU.add)
                p.ts(nrm[:, 6:7], nrm[:, 6:7], 0.5, None, ALU.mult)
                p.ts(nrm[:, 2:3], nrm[:, 6:7], COS_EPS, None, ALU.max)
                p.recip(nrm[:, 3:4], nrm[:, 2:3])
                if cos_fold:
                    p.tt(nrm[:, 4:5], nrm[:, 3:4], colset[:, j:j + 1],
                         ALU.mult)
                    src = nrm[:, 4:5]
                else:
                    src = nrm[:, 3:4]
                if negate_half2 and h2 == 1:
                    p.ts(nrm[:, 7:8], src, -1.0, None, ALU.mult)
                    src = nrm[:, 7:8]
                p.ts(ptok[:, c0:c0 + 64], ptok[:, c0:c0 + 64], src,
                     None, ALU.mult)

    def split_spill_T(db, de, tt):
        for (dst, second) in ((db, False), (de, True)):
            if second:
                p.tt(ptok[:], ptok[:], ptb[:], ALU.subtract)
            p.copy(ptb[:], ptok[:])
            for jc in range(4):
                p.transp(psTrS[:, jc * 128:(jc + 1) * 128],
                         ptb[:, jc * 128:(jc + 1) * 128], ident)
            p.copy(ptt[:], psTrS[:, 0:512])
            p.dma(dst.rearrange("(s q) t -> q s t", q=128)
                  [:, :, tt * 128:(tt + 1) * 128],
                  ptt[:].rearrange("q (s t) -> q s t", t=128), lane="sp")

    load_wT_hl(Wq)
    for tt in range(NT):
        proj3(tt)
        norm_cos(True, False)
        split_spill_T(qnb_d, qne_d, tt)
    load_wT_hl(Wk)
    for tt in range(NT):
        proj3(tt)
        norm_cos(False, True)
        split_spill_T(knb_d, kne_d, tt)
    load_wT_hl(Wv)
    for tt in range(NT):
        proj3(tt)
        p.copy(ptb[:], ptok[:])
        p.dma(vb_d[tt * 128:(tt + 1) * 128, :], ptb[:], lane="sp")
        p.tt(ptok[:], ptok[:], ptb[:], ALU.subtract)
        p.copy(ptb[:], ptok[:])
        p.dma(ve_d[tt * 128:(tt + 1) * 128, :], ptb[:], lane="sp")

    import os as _os
    PHB = int(_os.environ.get("L0_PHB", "1"))
    QB = 128
    kn1b = x_feat[0:64, 0:T]
    kn1e = x_feat[0:64, T:2 * T]
    kn2b = x_feat[0:64, 2 * T:3 * T]
    kn2e = x_feat[0:64, 3 * T:4 * T]
    vbs = x_feat[:, 4 * T:5 * T]
    ves = x_feat[:, 5 * T:6 * T]
    e1b = xe_feat[:, 0:T]
    e1e = xe_feat[:, T:2 * T]
    e2b = xe_feat[:, 2 * T:3 * T]
    e2e = xe_feat[:, 3 * T:4 * T]
    q1b = p.sb("q1b", [64, QB], BF16)
    q1e = p.sb("q1e", [64, QB], BF16)
    q2b = p.sb("q2b", [64, QB], BF16)
    q2e = p.sb("q2e", [64, QB], BF16)
    gst = p.sb("gst", [128, 512], F32)
    s1m = p.sb("s1m", [128, 512], F32)
    s2m = p.sb("s2m", [128, 512], F32)
    e1f = p.sb("e1f", [128, 512], F32)
    e2f = p.sb("e2f", [128, 512], F32)
    onesc2 = p.const_sb("onesc2", np.ones((128, 1), np.float32), BF16)
    psS1 = p.ps("psS1", [128, 512], F32)
    psS2 = p.ps("psS2", [128, 512], F32)
    psD = p.ps("psD", [128, 512], F32)
    psR = p.ps("psR", [1, 512], F32)
    psR2 = p.ps("psR2", [1, 256], F32)
    rrow = p.sb("rrow", [1, 4 * QB], F32)
    sum12 = p.sb("sum12", [1, 2 * QB], F32)
    rhib = p.sb("rhib", [1, QB], BF16)
    rlob = p.sb("rlob", [1, 2 * QB], BF16)
    rbc1 = p.sb("rbc1", [128, QB], F32)
    rbc2 = p.sb("rbc2", [128, QB], F32)
    ya = p.sb("ya", [128, QB], F32)
    yb = p.sb("yb", [128, QB], F32)
    sqy = p.sb("sqy", [128, QB], F32)
    yab = p.sb("yab", [128, QB], BF16)
    for j in range(NH if PHB else 0):
        gcol = colset[:, 5 + j:6 + j]
        p.dma(kn1b[:], knb_d[j * 128:j * 128 + 64, :], lane="sp")
        p.dma(kn1e[:], kne_d[j * 128:j * 128 + 64, :], lane="sp")
        p.dma(kn2b[:], knb_d[j * 128 + 64:(j + 1) * 128, :], lane="sp")
        p.dma(kn2e[:], kne_d[j * 128 + 64:(j + 1) * 128, :], lane="sp")
        for n in range(NT):
            p.dma(vbs[:, n * 128:(n + 1) * 128],
                  vb_d[n * 128:(n + 1) * 128, j * 128:(j + 1) * 128],
                  lane="sp")
            p.dma(ves[:, n * 128:(n + 1) * 128],
                  ve_d[n * 128:(n + 1) * 128, j * 128:(j + 1) * 128],
                  lane="sp")
        for qb in range(T // QB):
            q0 = qb * QB
            p.dma(q1b[:], qnb_d[j * 128:j * 128 + 64, q0:q0 + QB], lane="sp")
            p.dma(q1e[:], qne_d[j * 128:j * 128 + 64, q0:q0 + QB], lane="sp")
            p.dma(q2b[:], qnb_d[j * 128 + 64:(j + 1) * 128, q0:q0 + QB],
                  lane="sp")
            p.dma(q2e[:], qne_d[j * 128 + 64:(j + 1) * 128, q0:q0 + QB],
                  lane="sp")
            for ktg in range(NT // 4):
                for u in range(4):
                    kt = ktg * 4 + u
                    sl = slice(u * QB, (u + 1) * QB)
                    kb1 = kn1b[:, kt * 128:(kt + 1) * 128]
                    ke1 = kn1e[:, kt * 128:(kt + 1) * 128]
                    kb2 = kn2b[:, kt * 128:(kt + 1) * 128]
                    ke2 = kn2e[:, kt * 128:(kt + 1) * 128]
                    p.mm(psS1[:, sl], kb1, q1b[:], start=True, stop=False)
                    p.mm(psS1[:, sl], kb1, q1e[:], start=False, stop=False)
                    p.mm(psS1[:, sl], ke1, q1b[:], start=False, stop=True)
                    p.mm(psS2[:, sl], kb2, q2b[:], start=True, stop=False)
                    p.mm(psS2[:, sl], kb2, q2e[:], start=False, stop=False)
                    p.mm(psS2[:, sl], ke2, q2b[:], start=False, stop=True)
                    p.mm(psD[:, sl], kb1, q1b[:], start=True, stop=False)
                    p.mm(psD[:, sl], kb1, q1e[:], start=False, stop=False)
                    p.mm(psD[:, sl], ke1, q1b[:], start=False, stop=False)
                    p.mm(psD[:, sl], kb2, q2b[:], start=False, stop=False)
                    p.mm(psD[:, sl], kb2, q2e[:], start=False, stop=False)
                    p.mm(psD[:, sl], ke2, q2b[:], start=False, stop=True)
                p.ts(gst[:], psD[:], gcol, None, ALU.mult)
                p.act(gst[:], gst[:], AF.Tanh)
                p.tt(s1m[:], psS1[:], gst[:], ALU.add)
                p.act(e1f[:], s1m[:], AF.Exp)
                p.tt(s2m[:], psS2[:], gst[:], ALU.add)
                p.act(e2f[:], s2m[:], AF.Exp, scale=-1.0)
                eb = e1b[:, ktg * 512:(ktg + 1) * 512]
                p.copy(eb, e1f[:], lane="act")
                p.tt(e1e[:, ktg * 512:(ktg + 1) * 512], e1f[:], eb,
                     ALU.subtract)
                eb2 = e2b[:, ktg * 512:(ktg + 1) * 512]
                p.copy(eb2, e2f[:], lane="act")
                p.tt(e2e[:, ktg * 512:(ktg + 1) * 512], e2f[:], eb2,
                     ALU.subtract)
                for u in range(4):
                    esl = slice(ktg * 512 + u * QB, ktg * 512 + (u + 1) * QB)
                    p.mm(psR[:, 0:QB], onesc2[:], e1b[:, esl],
                         start=(u == 0), stop=False)
                    p.mm(psR[:, 0:QB], onesc2[:], e1e[:, esl],
                         start=False, stop=(u == 3))
                for u in range(4):
                    esl = slice(ktg * 512 + u * QB, ktg * 512 + (u + 1) * QB)
                    p.mm(psR2[:, 0:QB], onesc2[:], e2b[:, esl],
                         start=(u == 0), stop=False)
                    p.mm(psR2[:, 0:QB], onesc2[:], e2e[:, esl],
                         start=False, stop=(u == 3))
                if ktg == 0:
                    p.copy(sum12[0:1, 0:QB], psR[:, 0:QB])
                    p.copy(sum12[0:1, QB:2 * QB], psR2[:, 0:QB])
                else:
                    p.tt(sum12[0:1, 0:QB], sum12[0:1, 0:QB],
                         psR[:, 0:QB], ALU.add)
                    p.tt(sum12[0:1, QB:2 * QB], sum12[0:1, QB:2 * QB],
                         psR2[:, 0:QB], ALU.add)
            p.copy(rrow[0:1, 0:2 * QB], sum12[0:1, 0:2 * QB])
            if j == 0 and qb == 0 and _os.environ.get("L0_DBGSUM"):
                dbgsum = nc.dram_tensor("dbgsum", [1, 256], F32,
                                        kind="ExternalOutput").ap()
                dbge1 = nc.dram_tensor("dbge1", [128, 512], F32,
                                       kind="ExternalOutput").ap()
                p.dma_out(dbgsum[:], rrow[0:1, 0:256])
                dbgco = p.sb("dbgco", [128, 512], F32)
                p.tt(dbgco[:], e1b[:, 2048:2560], e1e[:, 2048:2560], ALU.add)
                p.dma_out(dbge1[:], dbgco[:])
            p.recip(rrow[0:1, 0:QB], rrow[0:1, 0:QB])
            p.recip(rrow[0:1, QB:2 * QB], rrow[0:1, QB:2 * QB])
            bcast_rows_f32(p, ones1, psS1[:, 0:QB], rrow[0:1, 0:QB],
                           rhib, rlob, rbc1[:], QB)
            bcast_rows_f32(p, ones1, psS2[:, 0:QB], rrow[0:1, QB:2 * QB],
                           rhib, rlob, rbc2[:], QB)
            for (ps, ebuf, eebuf) in ((psS1, e1b, e1e), (psS2, e2b, e2e)):
                for kt in range(NT):
                    ktg, u = kt // 4, kt % 4
                    esl = slice(ktg * 512 + u * QB, ktg * 512 + (u + 1) * QB)
                    vsl = vbs[:, kt * 128:(kt + 1) * 128]
                    vesl = ves[:, kt * 128:(kt + 1) * 128]
                    p.mm(ps[:, 256:256 + QB], vsl, ebuf[:, esl],
                         start=(kt == 0), stop=False)
                    p.mm(ps[:, 256:256 + QB], vsl, eebuf[:, esl],
                         start=False, stop=False)
                    p.mm(ps[:, 256:256 + QB], vesl, ebuf[:, esl],
                         start=False, stop=(kt == NT - 1))
            p.tt(ya[:], psS1[:, 256:256 + QB], rbc1[:], ALU.mult)
            p.tt(yb[:], psS2[:, 256:256 + QB], rbc2[:], ALU.mult)
            p.stt(ya[:], yb[:], colset[:, 4:5], ya[:], ALU.mult, ALU.add)
            p.tt(sqy[:], ya[:], ya[:], ALU.mult)
            p.copy(yab[:], sqy[:])
            p.mm(psR[:, 2 * QB:3 * QB], onesc2[:], yab[:])
            p.ts(rrow[0:1, 2 * QB:3 * QB], psR[:, 2 * QB:3 * QB],
                 1.0 / 128.0, 1e-5, ALU.mult, ALU.add)
            p.act(rrow[0:1, 3 * QB:4 * QB], rrow[0:1, 2 * QB:3 * QB], AF.Sqrt)
            p.recip(rrow[0:1, 0:QB], rrow[0:1, 3 * QB:4 * QB])
            p.tt(rrow[0:1, QB:2 * QB], rrow[0:1, 2 * QB:3 * QB],
                 rrow[0:1, 0:QB], ALU.mult)
            p.tt(rrow[0:1, QB:2 * QB], rrow[0:1, QB:2 * QB],
                 rrow[0:1, 3 * QB:4 * QB], ALU.add)
            p.ts(rrow[0:1, QB:2 * QB], rrow[0:1, QB:2 * QB], 0.5, None,
                 ALU.mult)
            p.recip(rrow[0:1, 2 * QB:3 * QB], rrow[0:1, QB:2 * QB])
            bcast_rows_f32(p, ones1, psS1[:, 0:QB], rrow[0:1, 2 * QB:3 * QB],
                           rhib, rlob, rbc1[:], QB)
            p.tt(ya[:], ya[:], rbc1[:], ALU.mult)
            p.ts(ya[:], ya[:], subw_sb[:], None, ALU.mult)
            p.dma_out(yT[j * 128:(j + 1) * 128, q0:q0 + QB], ya[:], lane="sp")
    p.build()
    return nc, p


def l0attn_in_maps(x_full, inp):
    maps = []
    for c in range(NCORES):
        b, g = c // 2, c % 2
        sl = slice(g * 512, (g + 1) * 512)
        maps.append({
            "x_all": np.ascontiguousarray(x_full[b]),
            "Wq512": np.ascontiguousarray(inp["L0_attn_Wq"][sl, :]),
            "Wk512": np.ascontiguousarray(inp["L0_attn_Wk"][sl, :]),
            "Wv512": np.ascontiguousarray(inp["L0_attn_Wv"][sl, :]),
            "lam_q1": inp["L0_attn_lambda_q1"].reshape(1, 64),
            "lam_k1": inp["L0_attn_lambda_k1"].reshape(1, 64),
            "lam_q2": inp["L0_attn_lambda_q2"].reshape(1, 64),
            "lam_k2": inp["L0_attn_lambda_k2"].reshape(1, 64),
            "delta_gain4": np.ascontiguousarray(
                inp["L0_attn_delta_gain"][g * 4:(g + 1) * 4].reshape(1, 4)),
            "cos_head_delta": inp["L0_attn_cos_head_delta"].reshape(1, 8),
            "chd_my": np.ascontiguousarray(
                inp["L0_attn_cos_head_delta"][g * 4:(g + 1) * 4].reshape(1, 4)),
            "cos_raw": inp["L0_attn_cos_logit_scale_raw"].reshape(1, 1),
            "subln_w": inp["L0_attn_subln_weight"].reshape(128, 1),
        })
    return maps



def build_l3attn_nc():
    nc = new_nc()
    NH, HD = 8, 64
    x = nc.dram_tensor("x_all", [T, D], F32, kind="ExternalInput").ap()
    Wq = nc.dram_tensor("Wq512", [D, 512], F32, kind="ExternalInput").ap()
    Wk = nc.dram_tensor("Wk512", [D, 512], F32, kind="ExternalInput").ap()
    Wv = nc.dram_tensor("Wv512", [D, 512], F32, kind="ExternalInput").ap()
    Wkpre = nc.dram_tensor("Wk_pre", [D, 512], F32, kind="ExternalInput").ap()
    kpsc = nc.dram_tensor("kpre_scale", [128, 1], F32, kind="ExternalInput").ap()
    yT = nc.dram_tensor("yT_my", [512, T], F32, kind="ExternalOutput").ap()

    p = Prog(nc, "l3attn")
    ident, identf = load_ident(p)
    NT = T // 128
    x_feat = build_feat(p, x, T, ident)
    psTrS = p.ps("psTr_sh", [128, 1024], BF16)
    psF = p.ps("psF_sh", [128, 512], F32)

    def load_w(W, nm):
        wsb = p.sb(nm, [128, NCH * 512], BF16)
        for c in range(NCH):
            p.dma(wsb[:, c * 512:(c + 1) * 512],
                  W[c * 128:(c + 1) * 128, :], lane="poold")
        return wsb

    q_feat = p.sb("q_feat", [128, 4 * T], BF16)
    k_tok = p.sb("k_tok", [128, 4 * T], BF16)
    v_tok = p.sb("v_tok", [128, 4 * T], BF16)
    m0 = p.sb("m0", [128, 128], F32)
    e0 = p.sb("e0", [128, 128], F32)
    r0 = p.sb("r0", [128, 128], F32)
    fb = p.sb("fb", [128, 128], BF16)

    wq = load_w(Wq, "w3sh")

    fscr = p.sb("fscr", [128, 128], BF16)

    def proj_feat(wsb, dst, tt, featmap, dst_tok=None):
        for sub in range(4):
            for c in range(NCH):
                p.mm(psF[:, 0:128],
                     wsb[:, c * 512 + sub * 128: c * 512 + (sub + 1) * 128],
                     x_feat[:, c * T + tt * 128: c * T + (tt + 1) * 128],
                     start=(c == 0), stop=(c == NCH - 1))
            if dst is not None:
                dap = dst[:, sub * T + tt * 128: sub * T + (tt + 1) * 128]
            else:
                dap = fscr[:]
            if featmap:
                p.ts(m0[:, 0:128], psF[:, 0:128], 0.0, None, ALU.min)
                p.act(e0[:, 0:128], m0[:, 0:128], AF.Exp)
                p.ts(r0[:, 0:128], psF[:, 0:128], 0.0, None, ALU.max)
                p.tt(dap, r0[:, 0:128], e0[:, 0:128], ALU.add)
            else:
                p.copy(dap, psF[:, 0:128])
            if dst_tok is not None:
                p.copy(fb[:, 0:128], dap)
                p.transp(psTrS[:, sub * 128:(sub + 1) * 128], fb[:, 0:128],
                         ident)
        if dst_tok is not None:
            p.copy(dst_tok[:, tt * 512:(tt + 1) * 512], psTrS[:, 0:512])

    for tt in range(NT):
        proj_feat(wq, q_feat, tt, True)
    wk = load_w(Wk, "w3sh")
    for tt in range(NT):
        proj_feat(wk, None, tt, True, dst_tok=k_tok)
    wv = load_w(Wv, "w3sh")
    for tt in range(NT):
        for c in range(NCH):
            p.mm(psF[:], x_feat[:, c * T + tt * 128: c * T + (tt + 1) * 128],
                 wv[:, c * 512:(c + 1) * 512],
                 start=(c == 0), stop=(c == NCH - 1))
        p.copy(v_tok[:, tt * 512:(tt + 1) * 512], psF[:])

    wkp = load_w(Wkpre, "w3sh")
    kpsc_sb = p.sb("kpsc_sb", [128, 1], F32)
    p.dma(kpsc_sb[:], kpsc[:])
    kpre_sum = p.sb("kpre_sum", [128, NT * HD], BF16)
    kpf = p.sb("kpf", [128, 512], BF16)
    for tt in range(NT):
        for c in range(NCH):
            p.mm(psF[:], x_feat[:, c * T + tt * 128: c * T + (tt + 1) * 128],
                 wkp[:, c * 512:(c + 1) * 512],
                 start=(c == 0), stop=(c == NCH - 1))
        for sb4 in range(4):
            sl = slice(sb4 * 128, (sb4 + 1) * 128)
            p.ts(m0[:], psF[:, sl], 0.0, None, ALU.min)
            p.act(e0[:], m0[:], AF.Exp)
            p.ts(r0[:], psF[:, sl], 0.0, None, ALU.max)
            p.tt(kpf[:, sl], r0[:], e0[:], ALU.add)
        ks = kpre_sum[:, tt * HD:(tt + 1) * HD]
        p.tt(ks, kpf[:, 0:64], kpf[:, 64:128], ALU.add)
        for blk_ in range(2, 8):
            p.tt(ks, ks, kpf[:, blk_ * 64:(blk_ + 1) * 64], ALU.add)
        p.ts(ks, ks, kpsc_sb[:], None, ALU.mult)

    trimask = np.triu(np.ones((128, 128), np.float32))
    tmask = p.const_sb("tmask", trimask, BF16)
    S_bf = p.sb("S_bf", [128, NH * HD], BF16)
    S_f32 = p.sb("S_f32", [64, NH * HD], F32)
    kacc = p.sb("kacc", [128, HD], F32)
    dstat = p.sb("dstat", [128, 2], F32)
    zcol = p.sb("zcol", [128, 1], F32)
    attnb = p.sb("attnb", [128, 128], BF16)
    khf = p.sb("khf", [128, 128], BF16)
    qtk = p.sb("qtk", [128, 64], BF16)
    outtok = p.sb("outtok", [128, HD], F32)
    psA2 = p.ps("psA2", [128, 128], F32)
    psOut = p.ps("psOut", [128, HD], F32)
    psS = p.ps("psSst", [64, HD], F32)
    ZEPS = 1e-6
    psS8 = p.ps("psS8", [64, 8 * HD], F32)
    for ch in range(NT):
        c0 = ch * 128
        p.copy(kacc[:], kpre_sum[:, ch * HD:(ch + 1) * HD])
        for j in range(NH):
            sub, half = j // 2, j % 2
            qh = q_feat[half * 64:(half + 1) * 64, sub * T:(sub + 1) * T]
            p.transp(psTrS[0:64, 0:128],
                     k_tok[:, ch * 512 + j * 64: ch * 512 + (j + 1) * 64],
                     ident)
            p.copy(khf[0:64, :] if half == 0 else khf[64:128, :],
                   psTrS[0:64, 0:128])
            kh2 = khf[0:64, :] if half == 0 else khf[64:128, :]
            p.transp(psTrS[:, 128:192], qh[:, c0:c0 + 128],
                     ident[half * 64:(half + 1) * 64,
                           half * 64:(half + 1) * 64])
            p.copy(qtk[:], psTrS[:, 128:192])
            p.tt(kacc[:], kacc[:],
                 k_tok[:, ch * 512 + j * 64: ch * 512 + (j + 1) * 64],
                 ALU.add)
            p.mm(psA2[:], kh2, qh[:, c0:c0 + 128])
            p.tt(attnb[:], psA2[:], tmask[:], ALU.mult)
            if ch == 0:
                p.mm(psOut[:], attnb[:],
                     v_tok[:, ch * 512 + j * 64: ch * 512 + (j + 1) * 64],
                     start=True, stop=True)
            else:
                p.mm(psOut[:], attnb[:],
                     v_tok[:, ch * 512 + j * 64: ch * 512 + (j + 1) * 64],
                     start=True, stop=False)
                p.mm(psOut[:], qh[:, c0:c0 + 128],
                     S_bf[half * 64:(half + 1) * 64,
                          j * HD:(j + 1) * HD],
                     start=False, stop=True)
            p.stt(outtok[:], qtk[:], 1.0, kacc[:], ALU.mult, ALU.mult,
                  accum_out=dstat[:, 0:1])
            p.ts(dstat[:, 1:2], dstat[:, 0:1], 1.0, ZEPS, ALU.mult, ALU.add)
            p.recip(zcol[:], dstat[:, 1:2])
            p.ts(outtok[:, 0:HD], psOut[:], zcol[:], None, ALU.mult)
            p.mm(psS8[:, j * HD:(j + 1) * HD],
                 k_tok[:, ch * 512 + j * 64: ch * 512 + (j + 1) * 64],
                 v_tok[:, ch * 512 + j * 64: ch * 512 + (j + 1) * 64],
                 start=True, stop=True)
            if ch == 0:
                p.copy(S_f32[:, j * HD:(j + 1) * HD],
                       psS8[:, j * HD:(j + 1) * HD])
            else:
                p.tt(S_f32[:, j * HD:(j + 1) * HD],
                     S_f32[:, j * HD:(j + 1) * HD],
                     psS8[:, j * HD:(j + 1) * HD], ALU.add)
            p.copy(S_bf[0:64, j * HD:(j + 1) * HD],
                   S_f32[:, j * HD:(j + 1) * HD])
            p.copy(S_bf[64:128, j * HD:(j + 1) * HD],
                   S_f32[:, j * HD:(j + 1) * HD])
            p.copy(fb[:, 0:64], outtok[:, 0:HD])
            p.transp(psTrS[0:64, 0:128], fb[:, 0:64], ident)
            p.copy(m0[0:64, 0:128], psTrS[0:64, 0:128])
            p.dma_out(yT[j * 64:(j + 1) * 64, c0:c0 + 128], m0[0:64, 0:128],
                      lane="sp")
    p.build()
    return nc, p


def l3attn_in_maps(x_full, Wq, Wk, Wv):
    maps = []
    for c in range(NCORES):
        b, g = c // 2, c % 2
        sl = slice(g * 512, (g + 1) * 512)
        maps.append({
            "x_all": np.ascontiguousarray(x_full[b]),
            "Wq512": np.ascontiguousarray(Wq[:, sl]),
            "Wk512": np.ascontiguousarray(Wk[:, sl]),
            "Wv512": np.ascontiguousarray(Wv[:, sl]),
            "Wk_pre": (np.zeros((D, 512), np.float32) if g == 0
                       else np.ascontiguousarray(Wk[:, 0:512])),
            "kpre_scale": np.full((128, 1), float(g), np.float32),
        })
    return maps



def build_l2ffn_nc():
    nc = new_nc()
    NU = 16
    NE = 65536
    x = nc.dram_tensor("x_my", [TPC, D], F32, kind="ExternalInput").ap()
    wqTr = nc.dram_tensor("w_qTr", [16, NCH, 128, 128], F32,
                          kind="ExternalInput").ap()
    keysT = nc.dram_tensor("keysT", [NU, 128, 256], F32,
                           kind="ExternalInput").ap()
    edown = nc.dram_tensor("e_down", [NE, D], F32, kind="ExternalInput").ap()
    eup = nc.dram_tensor("e_up", [NE, D], F32, kind="ExternalInput").ap()
    nw = nc.dram_tensor("norm_w", [1, D], F32, kind="ExternalInput").ap()
    xo = nc.dram_tensor("xo_my", [TPC, D], F32, kind="ExternalOutput").ap()

    p = Prog(nc, "l2ffn")
    ident, identf = load_ident(p)
    psTrS = p.ps("psTr_sh", [128, 1024], BF16)
    psF = p.ps("psF_sh", [128, 512], F32)

    ones1 = p.const_sb("ones1f2", np.ones((1, 128), np.float32), BF16)
    nwrow = p.sb("nwrow", [1, D], BF16)
    p.dma(nwrow[:], nw[:], lane="poold")
    nw_bc = p.sb("nw_bc", [128, D], F32)
    for jj in range(2):
        p.mm(psF[:], ones1[:], nwrow[0:1, jj * 512:(jj + 1) * 512])
        p.copy(nw_bc[:, jj * 512:(jj + 1) * 512], psF[:])

    g0 = p.sb("g0", [128, 8 * D], BF16)
    g1 = p.sb("g1", [128, 8 * D], BF16)
    kb = p.sb("kb", [128, 16 * 256], BF16)
    ke = p.sb("ke", [128, 16 * 256], BF16)
    kstage = g0[:].bitcast(F32)
    ketmp = g1[:].bitcast(F32)
    p.dma(kstage.rearrange("p (u i) -> p u i", i=256),
          keysT.rearrange("u p i -> p u i"))
    p.copy(kb[:], kstage)
    p.tt(ketmp, kstage, kb[:], ALU.subtract)
    p.copy(ke[:], ketmp)

    HT2 = TPC // 2
    NTH = HT2 // 128
    x_tok = p.sb("x_tok", [128, 1024], F32)
    xnf = p.sb("xnf", [128, 1024], F32)
    stat = p.sb("rstat", [128, 4], F32)
    xn_tok = p.sb("xn_tok", [128, NTH * D], BF16)
    xnT_b = p.sb("xnT_b", [128, NCH * HT2], BF16)
    xnT_e = p.sb("xnT_e", [128, NCH * HT2], BF16)
    xeb = p.sb("xeb", [128, 1024], BF16)
    RMSEPS = 1e-6

    wf32 = p.sb("wf32", [128, 1024], F32)
    wqb = p.sb("wqb", [128, NU * 1024], BF16)
    wqe = p.sb("wqe", [128, NU * 1024], BF16)
    web = p.sb("web", [128, 1024], BF16)
    for u in range(NU):
        p.dma(wf32[:].rearrange("p (c q) -> p c q", q=128),
              wqTr[u].rearrange("c p q -> p c q"), lane="sp")
        p.copy(wqb[:, u * 1024:(u + 1) * 1024], wf32[:])
        p.tt(wf32[:], wf32[:], wqb[:, u * 1024:(u + 1) * 1024], ALU.subtract)
        p.copy(wqe[:, u * 1024:(u + 1) * 1024], wf32[:])

    qb = p.sb("qb", [128, NU * 128], BF16)
    qe = p.sb("qe", [128, NU * 128], BF16)
    qf = p.sb("qf", [128, 128], F32)
    psQ = p.ps("psQ2", [128, 128], F32)
    psS = p.ps("psS3", [128, 256], F32)
    sims = p.sb("sims", [128, 256], F32)
    t8a = p.sb("t8a", [128, 8], F32)
    t8b = p.sb("t8b", [128, 8], F32)
    i8a = p.sb("i8a", [128, 8], U16)
    i8b = p.sb("i8b", [128, 8], U16)
    i8af = p.sb("i8af", [128, 8], F32)
    i8bf = p.sb("i8bf", [128, 8], F32)
    allsc = p.sb("allsc", [128, 64], F32)
    fic = p.sb("fic", [128, 64], F32)
    fs8 = p.sb("fs8", [128, 8], F32)
    pk8 = p.sb("pk8", [128, 8], U16)
    pk8f = p.sb("pk8f", [128, 8], F32)
    fi8 = p.sb("fi8", [128, 8], F32)
    fii = p.sb("fii", [128, 8], I32)
    wts = p.sb("wts", [128, 8], F32)
    hid = p.sb("hid", [128, 8], F32)
    scr64 = p.sb("scr64", [128, 64], F32)
    iota64 = p.const_sb(
        "iota64", np.tile(np.arange(64, dtype=np.float32), (128, 1)), F32)
    acc = p.sb("accL2", [128, 1024], F32)
    s_t = p.sb("s_t2", [128, 1024], F32)

    for half in range(2):
        toff = half * HT2
        for tt in range(NTH):
            t_g = toff + tt * 128
            p.dma(x_tok[:], x[t_g:t_g + 128, :])
            p.tt(xnf[:], x_tok[:], x_tok[:], ALU.mult)
            p.red(stat[:, 0:1], xnf[:], ALU.add)
            p.ts(stat[:, 1:2], stat[:, 0:1], 1.0 / D, RMSEPS, ALU.mult,
                 ALU.add)
            p.act(stat[:, 2:3], stat[:, 1:2], AF.Sqrt)
            p.recip(stat[:, 3:4], stat[:, 2:3])
            p.ts(xnf[:], x_tok[:], stat[:, 3:4], None, ALU.mult)
            p.tt(xnf[:], xnf[:], nw_bc[:], ALU.mult)
            p.copy(xn_tok[:, tt * D:(tt + 1) * D], xnf[:])
            p.tt(x_tok[:], xnf[:], xn_tok[:, tt * D:(tt + 1) * D],
                 ALU.subtract)
            p.copy(xeb[:], x_tok[:])
            for c in range(NCH):
                p.transp(psTrS[:, c * 128:(c + 1) * 128],
                         xn_tok[:, tt * D + c * 128: tt * D + (c + 1) * 128],
                         ident)
            for c in range(NCH):
                p.copy(xnT_b[:, c * HT2 + tt * 128: c * HT2 + (tt + 1) * 128],
                       psTrS[:, c * 128:(c + 1) * 128])
            for c in range(NCH):
                p.transp(psTrS[:, c * 128:(c + 1) * 128],
                         xeb[:, c * 128:(c + 1) * 128], ident)
            for c in range(NCH):
                p.copy(xnT_e[:, c * HT2 + tt * 128: c * HT2 + (tt + 1) * 128],
                       psTrS[:, c * 128:(c + 1) * 128])

        for tt in range(NTH):
            t_g = toff + tt * 128
            for u in range(NU):
                for c in range(NCH):
                    p.mm(psQ[:], wqb[:, u * 1024 + c * 128:
                                     u * 1024 + (c + 1) * 128],
                         xnT_b[:, c * HT2 + tt * 128: c * HT2 + (tt + 1) * 128],
                         start=(c == 0), stop=False)
                    p.mm(psQ[:], wqb[:, u * 1024 + c * 128:
                                     u * 1024 + (c + 1) * 128],
                         xnT_e[:, c * HT2 + tt * 128: c * HT2 + (tt + 1) * 128],
                         start=False, stop=False)
                    p.mm(psQ[:], wqe[:, u * 1024 + c * 128:
                                     u * 1024 + (c + 1) * 128],
                         xnT_b[:, c * HT2 + tt * 128: c * HT2 + (tt + 1) * 128],
                         start=False, stop=(c == NCH - 1))
                p.copy(qf[:], psQ[:])
                p.copy(qb[:, u * 128:(u + 1) * 128], qf[:])
                p.tt(qf[:], qf[:], qb[:, u * 128:(u + 1) * 128], ALU.subtract)
                p.copy(qe[:, u * 128:(u + 1) * 128], qf[:])
            for h in range(8):
                ua, ub_ = 2 * h, 2 * h + 1
                for (uu, tE, iE, iF) in ((ua, t8a, i8a, i8af),
                                         (ub_, t8b, i8b, i8bf)):
                    p.mm(psS[:], qb[:, uu * 128:(uu + 1) * 128],
                         kb[:, uu * 256:(uu + 1) * 256], start=True,
                         stop=False)
                    p.mm(psS[:], qe[:, uu * 128:(uu + 1) * 128],
                         kb[:, uu * 256:(uu + 1) * 256], start=False,
                         stop=False)
                    p.mm(psS[:], qb[:, uu * 128:(uu + 1) * 128],
                         ke[:, uu * 256:(uu + 1) * 256], start=False,
                         stop=True)
                    p.copy(sims[:], psS[:])
                    p.vmax8(tE[:], sims[:])
                    p.vmaxidx(iE[:], tE[:], sims[:])
                    p.copy(iF[:], iE[:])
                a3 = t8a[:].rearrange("p (a o) -> p a o", o=1)
                b3 = t8b[:].rearrange("p (o b) -> p o b", o=1)
                p.tt(allsc[:].rearrange("p (a b) -> p a b", b=8),
                     a3.to_broadcast([128, 8, 8]),
                     b3.to_broadcast([128, 8, 8]), ALU.add)
                p.ts(i8af[:], i8af[:], 256.0, None, ALU.mult)
                p.tt(fic[:].rearrange("p (a b) -> p a b", b=8),
                     i8af[:].rearrange("p (a o) -> p a o", o=1)
                     .to_broadcast([128, 8, 8]),
                     i8bf[:].rearrange("p (o b) -> p o b", o=1)
                     .to_broadcast([128, 8, 8]), ALU.add)
                p.vmax8(fs8[:], allsc[:])
                p.vmaxidx(pk8[:], fs8[:], allsc[:])
                p.copy(pk8f[:], pk8[:])
                for j in range(8):
                    p.stt(scr64[:], iota64[:], pk8f[:, j:j + 1], fic[:],
                          ALU.is_equal, ALU.mult, accum_out=fi8[:, j:j + 1])
                p.copy(fii[:], fi8[:])
                p.ts(wts[:], fs8[:], 0.0, None, ALU.max)
                for j in range(8):
                    p.indirect_gather(g0[:, j * D:(j + 1) * D], edown,
                                      fii[:, j:j + 1])
                for j in range(8):
                    p.stt(s_t[:, 0:D], g0[:, j * D:(j + 1) * D], 1.0,
                          xn_tok[:, tt * D:(tt + 1) * D], ALU.mult, ALU.mult,
                          accum_out=hid[:, j:j + 1])
                p.act(hid[:], hid[:], AF.Gelu)
                p.tt(hid[:], hid[:], wts[:], ALU.mult)
                for j in range(8):
                    p.indirect_gather(g1[:, j * D:(j + 1) * D], eup,
                                      fii[:, j:j + 1])
                for j in range(8):
                    if h == 0 and j == 0:
                        p.ts(acc[:], g1[:, 0:D], hid[:, 0:1], None, ALU.mult)
                    else:
                        p.stt(acc[:], g1[:, j * D:(j + 1) * D],
                              hid[:, j:j + 1], acc[:], ALU.mult, ALU.add)
            p.dma(x_tok[:], x[t_g:t_g + 128, :])
            p.tt(s_t[:], x_tok[:], acc[:], ALU.add)
            layernorm_store(p, [(0, s_t[:])],
                            lambda q_, t_g=t_g: xo[t_g:t_g + 128, :])
    p.build()
    return nc, p


def l2ffn_in_maps(x_full, inp):
    wq = inp["L2_ffn_w_q"]
    keys = inp["L2_ffn_keys"]
    keysT = np.ascontiguousarray(
        keys.transpose(0, 2, 3, 1).reshape(16, 128, 256))
    wqTr = np.ascontiguousarray(
        wq.reshape(16, 128, 8, 128).transpose(0, 2, 3, 1))
    maps = []
    for c in range(NCORES):
        b, h = c // 2, c % 2
        maps.append({
            "x_my": np.ascontiguousarray(x_full[b, h * TPC:(h + 1) * TPC, :]),
            "w_qTr": wqTr, "keysT": keysT,
            "e_down": inp["L2_ffn_expert_down"],
            "e_up": inp["L2_ffn_expert_up"],
            "norm_w": inp["L2_ffn_norm_w"].reshape(1, D),
        })
    return maps



_BUILD_CACHE = {}


def _cached(key, fn):
    if key not in _BUILD_CACHE:
        _BUILD_CACHE[key] = fn()
    return _BUILD_CACHE[key]


def _run(nc, maps):
    return run_bass_kernel_spmd(nc, maps, core_ids=list(range(NCORES))).results


def _attnproj_maps(y_full, x_full, W, bias, yfeat):
    maps = []
    for c in range(NCORES):
        b, h = c // 2, c % 2
        m = {
            "x_my": np.ascontiguousarray(x_full[b, h * TPC:(h + 1) * TPC, :]),
            "W": W, "bias": np.ascontiguousarray(
                np.asarray(bias).reshape(1, D)),
        }
        if yfeat:
            m["yT_my"] = np.ascontiguousarray(
                y_full[b][:, h * TPC:(h + 1) * TPC])
        else:
            m["y_my"] = np.ascontiguousarray(
                y_full[b][h * TPC:(h + 1) * TPC, :])
        maps.append(m)
    return maps


def kernel(**inputs):
    inp = {k: np.ascontiguousarray(np.asarray(v, dtype=np.float32))
           for k, v in inputs.items()}
    x = inp["L0_attn_x"]

    nc, _ = _cached("l0attn", build_l0attn_nc)
    res = _run(nc, l0attn_in_maps(x, inp))
    yT = [np.concatenate([res[2 * b]["yT_my"], res[2 * b + 1]["yT_my"]], 0)
          for b in range(B)]
    nc, _ = _cached("apTT", lambda: build_attnproj_nc(True, True, True))
    res = _run(nc, _attnproj_maps(yT, x, inp["L0_attn_Wo"],
                                  inp["L0_attn_bo"], True))
    x = gather_token_sharded(res)
    nc, _ = _cached("chanmixE", lambda: build_chanmix_nc(True))
    res = _run(nc, chanmix_in_maps(x, inp["L0_ffn_W_r"], inp["L0_ffn_W_k"],
                                   inp["L0_ffn_W_v"], inp["L0_ffn_mu_r"],
                                   inp["L0_ffn_mu_k"]))
    x = gather_token_sharded(res)
    nc, _ = _cached("l1attn", build_l1attn_nc)
    res = _run(nc, l1attn_in_maps(x, inp["L1_attn_Wqa"], inp["L1_attn_Wqb"],
                                  inp["L1_attn_Wkva"], inp["L1_attn_Wkvb"]))
    y = [np.concatenate([res[2 * b]["y_my"], res[2 * b + 1]["y_my"]], 1)
         for b in range(B)]
    nc, _ = _cached("apFF", lambda: build_attnproj_nc(False, False))
    res = _run(nc, _attnproj_maps(y, x, inp["L1_attn_Wout"],
                                  inp["L1_attn_bout"], False))
    x = gather_token_sharded(res)
    nc, _ = _cached("l1ffn", build_l1ffn_nc)
    res = _run(nc, tsharded_in_maps(x, {
        "sigs": inp["L1_ffn_tile_sigs"], "W": inp["L1_ffn_W"],
        "bias8": inp["L1_ffn_b"]}))
    x = gather_token_sharded(res)
    nc, _ = _cached("l2attn", build_l2attn_nc)
    res = _run(nc, l2attn_in_maps(x, inp["L2_attn_basis_w"],
                                  inp["L2_attn_core"],
                                  inp["L2_attn_head_residual"],
                                  inp["L2_attn_v_w"], inp["L2_attn_v_b"]))
    y = [np.concatenate([res[2 * b]["y_my"], res[2 * b + 1]["y_my"]], 1)
         for b in range(B)]
    nc, _ = _cached("apTF", lambda: build_attnproj_nc(True, False))
    res = _run(nc, _attnproj_maps(y, x, inp["L2_attn_o_w"],
                                  inp["L2_attn_o_b"], False))
    x = gather_token_sharded(res)
    nc, _ = _cached("l2ffn", build_l2ffn_nc)
    res = _run(nc, l2ffn_in_maps(x, inp))
    x = gather_token_sharded(res)
    nc, _ = _cached("l3attn", build_l3attn_nc)
    res = _run(nc, l3attn_in_maps(x, inp["L3_attn_Wq"], inp["L3_attn_Wk"],
                                  inp["L3_attn_Wv"]))
    yT = [np.concatenate([res[2 * b]["yT_my"], res[2 * b + 1]["yT_my"]], 0)
          for b in range(B)]
    nc, _ = _cached("apFT", lambda: build_attnproj_nc(False, True))
    res = _run(nc, _attnproj_maps(yT, x, inp["L3_attn_Wo"],
                                  inp["L3_attn_bo"], True))
    x = gather_token_sharded(res)
    nc, _ = _cached("chanmix", build_chanmix_nc)
    res = _run(nc, chanmix_in_maps(x, inp["L3_ffn_W_r"], inp["L3_ffn_W_k"],
                                   inp["L3_ffn_W_v"], inp["L3_ffn_mu_r"],
                                   inp["L3_ffn_mu_k"]))
    x = gather_token_sharded(res)
    return x
